# version 04
# baseline (speedup 1.0000x reference)
import math
from contextlib import ExitStack

import numpy as np
import ml_dtypes
import concourse.bass as bass
import concourse.mybir as mybir
from concourse.bass_utils import run_bass_kernel_spmd
from concourse.aot_env import allow_ici_all_to_all

F32 = mybir.dt.float32
BF16 = mybir.dt.bfloat16
FP16 = mybir.dt.float16
AF = mybir.ActivationFunctionType
ALU = mybir.AluOpType
AX = mybir.AxisListType

NCORE = 8
S = 8192
E = 4096
H = 32
D = 128
I = 14336
TOK = S // NCORE
NE = E // 128
NI = I // 128
NOC = (E + 2 * D) // 128
QSCALE = float(D) ** -0.5
EPS = 1e-5
LN2_4 = math.log(2.0) / 4.0
SKIP_THRESH = 150.0
SLAB = 28
NSLAB = NI // SLAB

DEBUG = {}


def chunks_for(qt, l):
    i0 = 128 * qt
    slope_min = 2.0 ** (-(8 * l + 8) / 4.0)
    cut = SKIP_THRESH / slope_min
    c_lo = max(0, int(math.ceil((i0 - 511 - cut) / 512.0)))
    return list(range(c_lo, i0 // 512 + 1))


class Ctr:
    __slots__ = ("h", "v")

    def __init__(self, nc, stack, name):
        self.h = stack.enter_context(nc.semaphore(name))
        self.v = 0

    def inc(self, inst, n=1):
        inst.then_inc(self.h, n)
        self.v += n
        return self.v

    def wait(self, eng, v=None):
        v = self.v if v is None else v
        if v > 0:
            eng.wait_ge(self.h, v)


def build_nc():
    nc = bass.Bass("TRN2", target_bir_lowering=False)

    x_in = nc.dram_tensor("x_own", [TOK, E], F32, kind="ExternalInput").ap()
    wqkv = nc.dram_tensor("w_qkv", [E, E + 2 * D], F32, kind="ExternalInput").ap()
    wc_in = nc.dram_tensor("w_c", [E, E], F32, kind="ExternalInput").ap()
    lnw_in = nc.dram_tensor("ln_w", [E], F32, kind="ExternalInput").ap()
    w1_in = nc.dram_tensor("gate_w", [I, E], F32, kind="ExternalInput").ap()
    w2_in = nc.dram_tensor("up_w", [I, E], F32, kind="ExternalInput").ap()
    w3_in = nc.dram_tensor("down_w", [E, I], F32, kind="ExternalInput").ap()
    hids = nc.dram_tensor("hids", [1, 4], F32, kind="ExternalInput").ap()
    out = nc.dram_tensor("out", [TOK, E], F32, kind="ExternalOutput").ap()

    kv_loc = nc.dram_tensor("kv_loc", [TOK, 2 * D], BF16).ap()
    kv_all = nc.dram_tensor("kv_all", [S, 2 * D], BF16).ap()
    q_loc = nc.dram_tensor("q_loc", [E, TOK], BF16).ap()
    q_all = nc.dram_tensor("q_all", [E, TOK], BF16).ap()
    o_loc = nc.dram_tensor("o_loc", [S, 4 * D], BF16).ap()
    attn_all = nc.dram_tensor("attn_all", [S, 4 * D], BF16).ap()
    s_dram = nc.dram_tensor("s_dram", [1, TOK], F32).ap()

    dbg = {}
    for name, shape, dt in DEBUG.get("outs", []):
        dbg[name] = nc.dram_tensor(name, shape, dt, kind="ExternalOutput").ap()

    ident_b = nc.inline_tensor(np.eye(128, dtype=ml_dtypes.bfloat16), "c_identb").ap()
    ident_h = nc.inline_tensor(np.eye(128, dtype=np.float16), "c_identh").ap()
    ident_f = nc.inline_tensor(np.eye(128, dtype=np.float32), "c_identf").ap()
    iota_c = nc.inline_tensor(np.arange(128, dtype=np.float32)[:, None], "c_iota").ap()
    mt_np = (np.arange(128)[:, None] <= np.arange(128)[None, :])
    maskT_c = nc.inline_tensor(mt_np.astype(ml_dtypes.bfloat16), "c_maskT").ap()
    ml_np = (np.arange(128)[:, None] >= np.arange(128)[None, :])
    maskL_c = nc.inline_tensor(ml_np.astype(ml_dtypes.bfloat16), "c_maskL").ap()
    onesr_c = nc.inline_tensor(np.ones((1, 128), dtype=np.float32), "c_onesr").ap()
    onescb_c = nc.inline_tensor(np.ones((128, 1), dtype=ml_dtypes.bfloat16), "c_onescb").ap()

    big = ExitStack()
    C = lambda name: Ctr(nc, big, name)
    sd = C("sd"); gd = C("gd"); st = C("st"); pt = C("pt"); ev = C("ev"); ccs = C("ccs")
    q_mm = C("q_mm"); q_ev = C("q_ev")
    a_sc = C("a_sc"); a_ex = C("a_ex"); a_tr = C("a_tr"); a_ev = C("a_ev")
    a_av = C("a_av"); a_bv = C("a_bv"); a_rc = C("a_rc"); a_rec = C("a_rec"); a_os = C("a_os"); a_ms = C("a_ms")
    t_tr = C("t_tr"); t_wg = C("t_wg"); t_mm = C("t_mm"); t_si = C("t_si")
    t_ym = C("t_ym"); t_dn = C("t_dn"); t_ad = C("t_ad"); t_st = C("t_st")

    sb = lambda name, shape, dt: big.enter_context(nc.sbuf_tensor(name, shape, dt))
    identb = sb("identb", [128, 128], BF16)
    identh = sb("identh", [128, 128], FP16)
    identf = sb("identf", [128, 128], F32)
    iotai = sb("iotai", [128, 1], F32)
    maskT = sb("maskT", [128, 128], BF16)
    maskL = sb("maskL", [128, 128], BF16)
    onesr = sb("onesr", [1, 128], F32)
    onescb = sb("onescb", [128, 1], BF16)
    lnw_sb = sb("lnw_sb", [128, NE], F32)
    slope_bc = sb("slope_bc", [128, 4], F32)
    nslope_bc = sb("nslope_bc", [128, 4], F32)
    wvec = sb("wvec", [128, 16], F32)
    s_bc = sb("s_bc", [128, TOK], F32)
    s_cols = sb("s_cols", [128, 8], F32)
    cvals = sb("cvals", [128, 2], F32)
    xn_fm = sb("xn_fm", [128, NE, TOK], BF16)

    def xn_ap(k, c0, n):
        return bass.AP(xn_fm, k * TOK + c0, [[NE * TOK, 128], [1, n]])

    def ecopy(eng, out_ap, in_ap):
        return nc.scalar.copy(out_ap, in_ap) if eng is nc.scalar else eng.tensor_copy(out_ap, in_ap)

    sd.inc(nc.sync.dma_start(out=identb[:, :], in_=ident_b), 16)
    sd.inc(nc.sync.dma_start(out=identh[:, :], in_=ident_h), 16)
    sd.inc(nc.sync.dma_start(out=identf[:, :], in_=ident_f), 16)
    sd.inc(nc.sync.dma_start(out=iotai[:, :], in_=iota_c), 16)
    sd.inc(nc.sync.dma_start(out=maskT[:, :], in_=maskT_c), 16)
    sd.inc(nc.sync.dma_start(out=maskL[:, :], in_=maskL_c), 16)
    sd.inc(nc.sync.dma_start(out=onesr[:, :], in_=onesr_c), 16)
    sd.inc(nc.sync.dma_start(out=onescb[:, :], in_=onescb_c), 16)
    gd.inc(nc.gpsimd.dma_start(out=lnw_sb[:, :],
                               in_=bass.AP(lnw_in.tensor, 0, [[1, 128], [128, NE]]),
                               allow_slow_non_contiguous=True), 16)

    with nc.psum_tensor("ps0", [128, 512], F32) as ps0, \
         nc.sbuf_tensor("hrow", [1, 4], F32) as hrow, \
         nc.sbuf_tensor("srow4", [1, 4], F32) as srow4, \
         nc.sbuf_tensor("wtmp", [128, 1], F32) as wtmp:
        sd.inc(nc.sync.dma_start(out=hrow[:, :], in_=hids), 16)
        st.inc(nc.vector.memset(cvals[:, 0:1], -LN2_4))
        st.wait(nc.vector)
        st.inc(nc.vector.memset(cvals[:, 1:2], EPS))
        sd.wait(nc.scalar); gd.wait(nc.scalar); st.wait(nc.scalar)
        st.inc(nc.scalar.activation(srow4[:, :], hrow[:, :], AF.Exp,
                                    bias=cvals[0:1, 0:1], scale=-LN2_4))
        st.wait(nc.tensor); sd.wait(nc.tensor)
        st.inc(nc.tensor.matmul(ps0[:, 0:4], onesr[0:1, :], srow4[:, :],
                                start=True, stop=True))
        st.wait(nc.vector)
        st.inc(nc.vector.tensor_copy(slope_bc[:, :], ps0[:, 0:4]))
        st.wait(nc.vector)
        st.inc(nc.vector.tensor_scalar_mul(nslope_bc[:, :], slope_bc[:, :], -1.0))
        for l in range(4):
            for s_ in range(4):
                off = -64.0 if l == 0 else float(128 * s_ - 256)
                st.wait(nc.vector)
                st.inc(nc.vector.tensor_scalar(wtmp[:, :], iotai[:, :], off,
                                               slope_bc[:, l:l + 1], ALU.add, ALU.mult))
                st.wait(nc.scalar)
                st.inc(nc.scalar.activation(wvec[:, 4 * l + s_:4 * l + s_ + 1],
                                            wtmp[:, :], AF.Exp))
        nc.all_engine_barrier()

        with nc.sbuf_tensor("xt", [128, 2, E], F32) as xt, \
             nc.sbuf_tensor("sqscr", [128, E], BF16) as sqscr, \
             nc.sbuf_tensor("stat", [128, 10], F32) as stat, \
             nc.sbuf_tensor("xnt", [128, E], BF16) as xnt, \
             nc.sbuf_tensor("s8", [8, 128], F32) as s8, \
             nc.sbuf_tensor("srowT", [1, TOK], F32) as srowT, \
             nc.psum_tensor("psT1a", [128, 1024], BF16) as psT1a, \
             nc.psum_tensor("psT1b", [128, 1024], BF16) as psT1b:
            c_sum = stat[:, 0:1]; c_ssq = stat[:, 1:2]; c_m = stat[:, 2:3]
            c_e2 = stat[:, 3:4]; c_m2 = stat[:, 4:5]; c_var = stat[:, 5:6]
            c_ln = stat[:, 6:7]; c_r = stat[:, 7:8]; c_rr = stat[:, 8:9]
            c_mx = stat[:, 9:10]
            xn_done_st = [0] * 8
            bi = 0
            for tt in range(8):
                b = tt % 2
                if tt >= 2:
                    nc.sync.wait_ge(st.h, xn_done_st[tt - 2])
                sd.inc(nc.sync.dma_start(out=xt[:, b, :],
                                         in_=x_in[128 * tt:128 * tt + 128, :]), 16)
                nc.vector.wait_ge(sd.h, sd.v)
                nc.scalar.wait_ge(sd.h, sd.v)
                st.inc(nc.vector.reduce_sum(c_sum, xt[:, b, :], AX.X))
                st.wait(nc.scalar)
                st.inc(nc.scalar.activation(sqscr[:, :], xt[:, b, :], AF.Square,
                                            accum_out=c_ssq))
                st.wait(nc.vector)
                st.inc(nc.vector.tensor_scalar_mul(c_m, c_sum, 1.0 / E))
                st.wait(nc.vector)
                st.inc(nc.vector.tensor_scalar_mul(c_e2, c_ssq, 1.0 / E))
                st.wait(nc.vector)
                st.inc(nc.vector.tensor_mul(c_m2, c_m, c_m))
                st.wait(nc.vector)
                st.inc(nc.vector.tensor_sub(c_var, c_e2, c_m2))
                st.wait(nc.scalar)
                st.inc(nc.scalar.activation(c_ln, c_var, AF.Ln, bias=cvals[:, 1:2]))
                st.wait(nc.scalar)
                st.inc(nc.scalar.activation(c_r, c_ln, AF.Exp, scale=-0.5))
                st.wait(nc.scalar)
                st.inc(nc.scalar.activation(c_rr, c_ln, AF.Exp, scale=-1.0))
                st.wait(nc.vector)
                st.inc(nc.vector.tensor_mul(c_mx, c_var, c_rr))
                st.wait(nc.scalar)
                st.inc(nc.scalar.activation(c_ln, c_mx, AF.Ln, bias=cvals[:, 1:2]))
                st.wait(nc.scalar)
                st.inc(nc.scalar.activation(s_cols[:, tt:tt + 1], c_ln, AF.Exp, scale=-0.5))
                st.wait(nc.vector)
                if tt >= 1:
                    nc.vector.wait_ge(pt.h, 4 * tt)
                st.inc(nc.vector.tensor_scalar(xnt[:, :], xt[:, b, :], c_m, c_r,
                                               ALU.subtract, ALU.mult))
                xn_done_st[tt] = st.v
                st.wait(nc.tensor)
                for g in range(4):
                    psb = psT1a if bi % 2 == 0 else psT1b
                    if bi >= 2:
                        nc.tensor.wait_ge(ev.h, bi - 1)
                    for k8 in range(8):
                        k = 8 * g + k8
                        ins = nc.tensor.transpose(psb[:, 128 * k8:128 * k8 + 128],
                                                  xnt[:, 128 * k:128 * k + 128],
                                                  identb[:, :])
                    pt.inc(ins)
                    eng = nc.vector if bi % 2 == 0 else nc.scalar
                    eng.wait_ge(pt.h, pt.v)
                    dst = bass.AP(xn_fm, TOK * 8 * g + 128 * tt,
                                  [[NE * TOK, 128], [TOK, 8], [1, 128]])
                    ev.inc(ecopy(eng,
                        dst, bass.AP(psb, 0, [[1024, 128], [128, 8], [1, 128]])))
                    bi += 1
            nc.tensor.wait_ge(st.h, st.v)
            st.inc(nc.tensor.transpose(ps0[0:8, 0:128], s_cols[:, :], identf[:, :]))
            st.wait(nc.vector)
            st.inc(nc.vector.tensor_copy(s8[:, :], ps0[0:8, 0:128]))
            st.wait(nc.sync)
            sd.inc(nc.sync.dma_start(out=bass.AP(s_dram.tensor, 0, [[128, 8], [1, 128]]),
                                     in_=s8[:, :]), 16)
            nc.sync.wait_ge(sd.h, sd.v)
            sd.inc(nc.sync.dma_start(out=srowT[:, :], in_=s_dram), 16)
            nc.tensor.wait_ge(sd.h, sd.v)
            for t2 in range(2):
                if t2 == 1:
                    nc.tensor.wait_ge(st.h, st.v)
                st.inc(nc.tensor.matmul(ps0[:, :], onesr[0:1, :],
                                        srowT[:, 512 * t2:512 * t2 + 512],
                                        start=True, stop=True))
                st.wait(nc.vector)
                st.inc(nc.vector.tensor_copy(s_bc[:, 512 * t2:512 * t2 + 512], ps0[:, :]))
            nc.all_engine_barrier()
            if "d_xn" in dbg:
                gd.inc(nc.gpsimd.dma_start(
                    out=dbg["d_xn"],
                    in_=bass.AP(xn_fm, 0, [[NE * TOK, 128], [1, NE * TOK]])), 16)
                nc.gpsimd.wait_ge(gd.h, gd.v)
                nc.all_engine_barrier()

    with ExitStack() as p2:
        wcb = p2.enter_context(nc.sbuf_tensor("wcb", [128, 2, NE, 128], BF16))
        qstg = p2.enter_context(nc.sbuf_tensor("qstg", [128, 2, 512], BF16))
        kvfm = p2.enter_context(nc.sbuf_tensor("kvfm", [128, 2, TOK], BF16))
        kvtok = p2.enter_context(nc.sbuf_tensor("kvtok", [128, 8, 256], BF16))
        psQ = [p2.enter_context(nc.psum_tensor(f"psQ{i}", [128, 512], F32)) for i in range(2)]
        psP2 = [p2.enter_context(nc.psum_tensor(f"psP2{i}", [128, 1024], BF16)) for i in range(2)]

        q_store_sd = {}
        nc.tensor.wait_ge(ev.h, ev.v)
        nc.tensor.wait_ge(st.h, st.v)
        for oc in range(NOC):
            b = oc % 2
            if oc >= 2:
                nc.gpsimd.wait_ge(q_mm.h, 2 * (oc - 2) + 2)
            src = bass.AP(wqkv.tensor, 128 * oc,
                          [[E + 2 * D, 128], [(E + 2 * D) * 128, NE], [1, 128]])
            gd.inc(nc.gpsimd.dma_start(out=wcb[:, b, :, :], in_=src), 16)
            for t in range(2):
                u2 = 2 * oc + t
                nc.tensor.wait_ge(gd.h, gd.v)
                if u2 >= 2:
                    nc.tensor.wait_ge(q_ev.h, u2 - 1)
                for k in range(NE):
                    ins = nc.tensor.matmul(psQ[t][:, :], wcb[:, b, k, :],
                                           xn_ap(k, 512 * t, 512),
                                           start=(k == 0), stop=(k == NE - 1))
                q_mm.inc(ins)
                nc.vector.wait_ge(q_mm.h, u2 + 1)
                if oc < 32:
                    if u2 >= 2 and (u2 - 2) in q_store_sd:
                        nc.vector.wait_ge(sd.h, q_store_sd[u2 - 2])
                    q_ev.inc(nc.vector.tensor_scalar_mul(qstg[:, u2 % 2, :],
                                                         psQ[t][:, :], QSCALE))
                    nc.sync.wait_ge(q_ev.h, u2 + 1)
                    row0 = 512 * (oc % 8) + 128 * (oc // 8)
                    sd.inc(nc.sync.dma_start(
                        out=bass.AP(q_loc.tensor, row0 * TOK + 512 * t,
                                    [[TOK, 128], [1, 512]]),
                        in_=qstg[:, u2 % 2, :]), 16)
                    q_store_sd[u2] = sd.v
                else:
                    q_ev.inc(nc.vector.tensor_copy(
                        bass.AP(kvfm, (oc - 32) * TOK + 512 * t, [[2 * TOK, 128], [1, 512]]),
                        psQ[t][:, :]))
        nc.all_engine_barrier()
        nc.tensor.wait_ge(q_ev.h, q_ev.v)
        for tt in range(8):
            bb = tt % 2
            psb = psP2[bb]
            if tt >= 2:
                nc.tensor.wait_ge(ev.h, ev.v - 1)
            nc.tensor.transpose(psb[:, 0:128],
                                bass.AP(kvfm, 128 * tt, [[2 * TOK, 128], [1, 128]]),
                                identb[:, :])
            ins = nc.tensor.transpose(psb[:, 128:256],
                                      bass.AP(kvfm, TOK + 128 * tt, [[2 * TOK, 128], [1, 128]]),
                                      identb[:, :])
            pt.inc(ins)
            nc.vector.wait_ge(pt.h, pt.v)
            ev.inc(nc.vector.tensor_copy(kvtok[:, tt, :], psb[:, 0:256]))
        nc.sync.wait_ge(ev.h, ev.v)
        sd.inc(nc.sync.dma_start(
            out=bass.AP(kv_loc.tensor, 0, [[256, 128], [256 * 128, 8], [1, 256]]),
            in_=kvtok[:, :, :]), 16)
        nc.sync.wait_ge(sd.h, sd.v)
        nc.all_engine_barrier()

    ccs.inc(nc.gpsimd.collective_compute(
        "AllGather", ALU.bypass, replica_groups=[list(range(NCORE))],
        ins=[kv_loc.opt()], outs=[kv_all.opt()]))
    with allow_ici_all_to_all():
        ccs.inc(nc.gpsimd.collective_compute(
            "AllToAll", ALU.bypass, replica_groups=[list(range(NCORE))],
            ins=[q_loc.opt()], outs=[q_all.opt()]))
    nc.gpsimd.wait_ge(ccs.h, ccs.v)
    nc.all_engine_barrier()
    if "d_kv" in dbg:
        gd.inc(nc.gpsimd.dma_start(out=dbg["d_kv"], in_=kv_all), 16)
        nc.gpsimd.wait_ge(gd.h, gd.v)
    if "d_q" in dbg:
        gd.inc(nc.gpsimd.dma_start(out=dbg["d_q"], in_=q_all), 16)
        nc.gpsimd.wait_ge(gd.h, gd.v)

    with ExitStack() as p4:
        qg = p4.enter_context(nc.sbuf_tensor("qg", [128, 4, 8, TOK], BF16))
        K_sb = p4.enter_context(nc.sbuf_tensor("K_sb", [128, S], BF16))
        V_sb = p4.enter_context(nc.sbuf_tensor("V_sb", [128, 64, 128], BF16))
        prob = p4.enter_context(nc.sbuf_tensor("prob", [128, 2, 512], BF16))
        ptb = p4.enter_context(nc.sbuf_tensor("ptb", [128, 2, 512], BF16))
        bvec = p4.enter_context(nc.sbuf_tensor("bvec", [128, 2, 4], F32))
        recb = p4.enter_context(nc.sbuf_tensor("recb", [128, 2], F32))
        ostg = p4.enter_context(nc.sbuf_tensor("ostg", [128, 2, 512], BF16))
        psS = [p4.enter_context(nc.psum_tensor(f"psS{i}", [128, 512], F32)) for i in range(2)]
        psPT = [p4.enter_context(nc.psum_tensor(f"psPT{i}", [128, 1024], BF16)) for i in range(2)]
        psO = [p4.enter_context(nc.psum_tensor(f"psO{i}", [128, 512], F32)) for i in range(2)]
        psD = [p4.enter_context(nc.psum_tensor(f"psD{i}", [128, 512], F32)) for i in range(2)]

        sd.inc(nc.sync.dma_start(out=K_sb[:, :], in_=kv_all[:, 0:128], transpose=True), 16)
        sd.inc(nc.sync.dma_start(
            out=V_sb[:, :, :],
            in_=bass.AP(kv_all.tensor, 128, [[256, 128], [256 * 128, 64], [1, 128]])), 16)
        for hl4 in range(4):
            sd.inc(nc.sync.dma_start(
                out=qg[:, hl4, :, :],
                in_=bass.AP(q_all.tensor, 128 * TOK * hl4,
                            [[TOK, 128], [512 * TOK, 8], [1, TOK]])), 16)
        nc.sync.wait_ge(sd.h, sd.v)
        nc.all_engine_barrier()

        units = []
        for qt in range(64):
            for hli in range(4):
                cl = chunks_for(qt, hli)
                for ci, c in enumerate(cl):
                    diag = (c == qt // 4)
                    ns = (qt % 4) + 1 if diag else 4
                    units.append((qt, hli, c, ns, diag, ci == 0, ci == len(cl) - 1))
        U = len(units)
        a_os_after = {}

        for u in range(U + 2):
            if u < U:
                qt, l, c, ns, diag, first, last = units[u]
                i0 = 128 * qt
                nc.tensor.wait_ge(a_ex.h, max(0, u - 1))
                ins = nc.tensor.matmul(psS[u % 2][:, 0:128 * ns],
                                       qg[:, l, qt // 8, 128 * (qt % 8):128 * (qt % 8) + 128],
                                       K_sb[:, 512 * c:512 * c + 128 * ns],
                                       start=True, stop=True)
                a_sc.inc(ins)
                nc.vector.wait_ge(a_ex.h, max(0, u - 1))
                if l == 0:
                    for s_ in range(ns):
                        jc = 512 * c + 128 * s_ + 64
                        ins = nc.vector.tensor_scalar(bvec[:, u % 2, s_:s_ + 1], iotai[:, :],
                                                      float(i0 - jc), nslope_bc[:, 0:1],
                                                      ALU.add, ALU.mult)
                else:
                    jc = 512 * c + 256
                    ins = nc.vector.tensor_scalar(bvec[:, u % 2, 0:1], iotai[:, :],
                                                  float(i0 - jc), nslope_bc[:, l:l + 1],
                                                  ALU.add, ALU.mult)
                a_bv.inc(ins)
                nc.scalar.wait_ge(a_sc.h, u + 1)
                nc.scalar.wait_ge(a_bv.h, u + 1)
                nc.scalar.wait_ge(a_tr.h, max(0, u - 1))
                if l == 0:
                    for s_ in range(ns):
                        ins = nc.scalar.activation(prob[:, u % 2, 128 * s_:128 * s_ + 128],
                                                   psS[u % 2][:, 128 * s_:128 * s_ + 128],
                                                   AF.Exp, bias=bvec[:, u % 2, s_:s_ + 1])
                else:
                    ins = nc.scalar.activation(prob[:, u % 2, 0:128 * ns],
                                               psS[u % 2][:, 0:128 * ns],
                                               AF.Exp, bias=bvec[:, u % 2, 0:1])
                a_ex.inc(ins)
            if 1 <= u <= U:
                v = u - 1
                qt, l, c, ns, diag, first, last = units[v]
                e = 4 * qt + l
                if diag:
                    nc.vector.wait_ge(a_ex.h, v + 1)
                    sdiag = ns - 1
                    a_ms.inc(nc.vector.tensor_mul(
                        prob[:, v % 2, 128 * sdiag:128 * sdiag + 128],
                        prob[:, v % 2, 128 * sdiag:128 * sdiag + 128],
                        maskL[:, :]))
                nc.tensor.wait_ge(a_ex.h, v + 1)
                if diag:
                    nc.tensor.wait_ge(a_ms.h, a_ms.v)
                nc.tensor.wait_ge(a_ev.h, max(0, v - 1))
                for s_ in range(ns):
                    ins = nc.tensor.transpose(psPT[v % 2][:, 128 * s_:128 * s_ + 128],
                                              prob[:, v % 2, 128 * s_:128 * s_ + 128],
                                              identb[:, :])
                a_tr.inc(ins)
                nc.vector.wait_ge(a_tr.h, v + 1)
                nc.vector.wait_ge(a_av.h, max(0, v - 1))
                for s_ in range(ns):
                    wcol = wvec[:, 4 * l:4 * l + 1] if l == 0 else wvec[:, 4 * l + s_:4 * l + s_ + 1]
                    dstp = ptb[:, v % 2, 128 * s_:128 * s_ + 128]
                    srcp = psPT[v % 2][:, 128 * s_:128 * s_ + 128]
                    ins = nc.vector.tensor_scalar(dstp, srcp, wcol, None, ALU.mult)
                a_ev.inc(ins)
            if u >= 2:
                v2 = u - 2
                qt, l, c, ns, diag, first, last = units[v2]
                e = 4 * qt + l
                if first:
                    nc.tensor.wait_ge(a_rec.h, max(0, e - 1))
                nc.tensor.wait_ge(a_ev.h, v2 + 1)
                for s_ in range(ns):
                    jt = 4 * c + s_
                    nc.tensor.matmul(psO[e % 2][:, 0:128],
                                     ptb[:, v2 % 2, 128 * s_:128 * s_ + 128],
                                     V_sb[:, jt, :],
                                     start=(first and s_ == 0), stop=(last and s_ == ns - 1))
                for s_ in range(ns):
                    ins = nc.tensor.matmul(psD[e % 2][:, 0:1],
                                           ptb[:, v2 % 2, 128 * s_:128 * s_ + 128],
                                           onescb[:, :],
                                           start=(first and s_ == 0), stop=(last and s_ == ns - 1))
                a_av.inc(ins)
                if last:
                    nc.vector.wait_ge(a_av.h, v2 + 1)
                    a_rc.inc(nc.vector.reciprocal(recb[:, e % 2:e % 2 + 1],
                                                  psD[e % 2][:, 0:1]))
                    nc.vector.wait_ge(a_rc.h, e + 1)
                    if l == 0 and qt >= 2:
                        nc.vector.wait_ge(a_os.h, a_os_after[qt - 2])
                    a_rec.inc(nc.vector.tensor_scalar(
                        ostg[:, qt % 2, 128 * l:128 * l + 128], psO[e % 2][:, 0:128],
                        recb[:, e % 2:e % 2 + 1], None, ALU.mult))
                    if l == 3:
                        nc.sync.wait_ge(a_rec.h, e + 1)
                        a_os.inc(nc.sync.dma_start(
                            out=bass.AP(o_loc.tensor, 128 * qt * 512, [[512, 128], [1, 512]]),
                            in_=ostg[:, qt % 2, :]), 16)
                        a_os_after[qt] = a_os.v
        nc.sync.wait_ge(a_os.h, a_os.v)
        nc.all_engine_barrier()

    with allow_ici_all_to_all():
        ccs.inc(nc.gpsimd.collective_compute(
            "AllToAll", ALU.bypass, replica_groups=[list(range(NCORE))],
            ins=[o_loc.opt()], outs=[attn_all.opt()]))
    nc.gpsimd.wait_ge(ccs.h, ccs.v)
    nc.all_engine_barrier()
    if "d_attn" in dbg:
        gd.inc(nc.gpsimd.dma_start(out=dbg["d_attn"], in_=attn_all), 16)
        nc.gpsimd.wait_ge(gd.h, gd.v)

    for k in range(NE):
        st.inc(nc.vector.scalar_tensor_tensor(
            bass.AP(xn_fm, k * TOK, [[NE * TOK, 128], [1, TOK]]),
            s_bc[:, :], lnw_sb[:, k:k + 1],
            bass.AP(xn_fm, k * TOK, [[NE * TOK, 128], [1, TOK]]),
            ALU.mult, ALU.mult))
    nc.all_engine_barrier()

    for th in range(2):
        with ExitStack() as pt6:
            OutAcc = pt6.enter_context(nc.sbuf_tensor(f"OutAcc{th}", [128, NE, 512], FP16))
            with ExitStack() as p6a:
                att_ld = p6a.enter_context(nc.sbuf_tensor(f"att_ld{th}", [128, 2, 8, 4, 128], BF16))
                attn_sb = p6a.enter_context(nc.sbuf_tensor(f"attn_sb{th}", [128, NE, 512], BF16))
                psT6 = [p6a.enter_context(nc.psum_tensor(f"psT6{th}_{i}", [128, 1024], BF16))
                        for i in range(2)]
                bi6 = 0
                ld_sd = {}
                for tt in range(4):
                    b = tt % 2
                    if tt >= 2:
                        nc.sync.wait_ge(pt.h, pt.v)
                    for hl4 in range(4):
                        sd.inc(nc.sync.dma_start(
                            out=att_ld[:, b, :, hl4, :],
                            in_=bass.AP(attn_all.tensor, (512 * th + 128 * tt) * 512 + 128 * hl4,
                                        [[512, 128], [1024 * 512, 8], [1, 128]])), 16)
                    ld_sd[tt] = sd.v
                    nc.tensor.wait_ge(sd.h, ld_sd[tt])
                    for g in range(4):
                        psb = psT6[bi6 % 2]
                        if bi6 >= 2:
                            nc.tensor.wait_ge(ev.h, ev.v - 1)
                        for h8 in range(8):
                            h = 8 * g + h8
                            ins = nc.tensor.transpose(
                                psb[:, 128 * h8:128 * h8 + 128],
                                att_ld[:, b, h % 8, h // 8, :], identb[:, :])
                        pt.inc(ins)
                        eng = nc.vector if bi6 % 2 == 0 else nc.scalar
                        eng.wait_ge(pt.h, pt.v)
                        dst = bass.AP(attn_sb, 512 * 8 * g + 128 * tt,
                                      [[NE * 512, 128], [512, 8], [1, 128]])
                        ev.inc(ecopy(eng,
                            dst, bass.AP(psb, 0, [[1024, 128], [128, 8], [1, 128]])))
                        bi6 += 1
                nc.all_engine_barrier()
                wcb2 = p6a.enter_context(nc.sbuf_tensor(f"wcb2{th}", [128, 2, NE, 128], BF16))
                psF6 = [p6a.enter_context(nc.psum_tensor(f"psF6{th}_{i}", [128, 512], F32))
                        for i in range(2)]
                nc.tensor.wait_ge(ev.h, ev.v)
                for oc in range(NE):
                    b = oc % 2
                    if oc >= 2:
                        nc.gpsimd.wait_ge(t_mm.h, t_mm.v - 1)
                    src = bass.AP(wc_in.tensor, 128 * oc,
                                  [[E, 128], [E * 128, NE], [1, 128]])
                    gd.inc(nc.gpsimd.dma_start(out=wcb2[:, b, :, :], in_=src), 16)
                    nc.tensor.wait_ge(gd.h, gd.v)
                    if oc >= 2:
                        nc.tensor.wait_ge(t_ad.h, t_ad.v - 1)
                    for hh in range(NE):
                        ins = nc.tensor.matmul(psF6[b][:, :], wcb2[:, b, hh, :],
                                               bass.AP(attn_sb, hh * 512,
                                                       [[NE * 512, 128], [1, 512]]),
                                               start=(hh == 0), stop=(hh == NE - 1))
                    t_mm.inc(ins)
                    nc.vector.wait_ge(t_mm.h, t_mm.v)
                    t_ad.inc(nc.vector.tensor_copy(
                        bass.AP(OutAcc, oc * 512, [[NE * 512, 128], [1, 512]]),
                        psF6[b][:, :]))
                nc.all_engine_barrier()
                if "d_acc1" in dbg:
                    gd.inc(nc.gpsimd.dma_start(
                        out=bass.AP(dbg["d_acc1"].tensor, th * NE * 512,
                                    [[2 * NE * 512, 128], [1, NE * 512]]),
                        in_=bass.AP(OutAcc, 0, [[NE * 512, 128], [1, NE * 512]])), 16)
                    nc.gpsimd.wait_ge(gd.h, gd.v)
                    nc.all_engine_barrier()
            with ExitStack() as p8:
                w1b = p8.enter_context(nc.sbuf_tensor(f"w1b{th}", [128, 2, NE, 128], BF16))
                w2b = p8.enter_context(nc.sbuf_tensor(f"w2b{th}", [128, 2, NE, 128], BF16))
                w1t = p8.enter_context(nc.sbuf_tensor(f"w1t{th}", [128, NE, 128], BF16))
                w2t = p8.enter_context(nc.sbuf_tensor(f"w2t{th}", [128, NE, 128], BF16))
                w3b = p8.enter_context(nc.sbuf_tensor(f"w3b{th}", [128, SLAB, 128], BF16))
                w3t = p8.enter_context(nc.sbuf_tensor(f"w3t{th}", [128, SLAB, 128], BF16))
                gs_sb = p8.enter_context(nc.sbuf_tensor(f"gs_sb{th}", [128, 2, 512], BF16))
                Y_sb = p8.enter_context(nc.sbuf_tensor(f"Y_sb{th}", [128, SLAB, 512], BF16))
                psT8 = [p8.enter_context(nc.psum_tensor(f"psT8{th}_{i}", [128, 1024], BF16))
                        for i in range(2)]
                psG = p8.enter_context(nc.psum_tensor(f"psG{th}", [128, 512], F32))
                psU = p8.enter_context(nc.psum_tensor(f"psU{th}", [128, 512], F32))
                psF8 = [p8.enter_context(nc.psum_tensor(f"psF8{th}_{i}", [128, 512], F32))
                        for i in range(2)]
                bi8 = 0
                tw_after_ic = {}

                def transpose_block(src3, bidx, n, wdst, ld_val):
                    nonlocal bi8
                    nb = (n + 7) // 8
                    for g in range(nb):
                        kk = list(range(8 * g, min(8 * g + 8, n)))
                        psb = psT8[bi8 % 2]
                        if bi8 >= 2:
                            nc.tensor.wait_ge(t_wg.h, t_wg.v - 1)
                        nc.tensor.wait_ge(gd.h, ld_val)
                        for j, k in enumerate(kk):
                            if bidx is None:
                                ins = nc.tensor.transpose(psb[:, 128 * j:128 * j + 128],
                                                          src3[:, k, :], identb[:, :])
                            else:
                                ins = nc.tensor.transpose(psb[:, 128 * j:128 * j + 128],
                                                          src3[:, bidx, k, :], identb[:, :])
                        t_tr.inc(ins)
                        eng = nc.vector if bi8 % 2 == 0 else nc.scalar
                        eng.wait_ge(t_tr.h, t_tr.v)
                        dst = bass.AP(wdst, 128 * 8 * g,
                                      [[wdst.shape[1] * 128, 128], [128, len(kk)], [1, 128]])
                        t_wg.inc(ecopy(eng,
                            dst, bass.AP(psb, 0, [[1024, 128], [128, len(kk)], [1, 128]])))
                        bi8 += 1

                nc.tensor.wait_ge(st.h, st.v)
                nc.tensor.wait_ge(t_ad.h, t_ad.v)
                for slab in range(NSLAB):
                    for icl in range(SLAB):
                        gi = SLAB * slab + icl
                        b = gi % 2
                        if gi >= 2:
                            nc.gpsimd.wait_ge(t_wg.h, tw_after_ic[gi - 2])
                        src1 = bass.AP(w1_in.tensor, 128 * gi * E,
                                       [[E, 128], [128, NE], [1, 128]])
                        gd.inc(nc.gpsimd.dma_start(out=w1b[:, b, :, :], in_=src1), 16)
                        ld1 = gd.v
                        src2 = bass.AP(w2_in.tensor, 128 * gi * E,
                                       [[E, 128], [128, NE], [1, 128]])
                        gd.inc(nc.gpsimd.dma_start(out=w2b[:, b, :, :], in_=src2), 16)
                        ld2 = gd.v
                        transpose_block(w1b, b, NE, w1t, ld1)
                        nc.tensor.wait_ge(t_wg.h, t_wg.v)
                        if gi >= 1:
                            nc.tensor.wait_ge(t_si.h, t_si.v)
                        for k in range(NE):
                            ins = nc.tensor.matmul(psG[:, :], w1t[:, k, :],
                                                   xn_ap(k, 512 * th, 512),
                                                   start=(k == 0), stop=(k == NE - 1))
                        t_mm.inc(ins)
                        nc.scalar.wait_ge(t_mm.h, t_mm.v)
                        if gi >= 2:
                            nc.scalar.wait_ge(t_ym.h, t_ym.v - 1)
                        t_si.inc(nc.scalar.activation(gs_sb[:, b, :], psG[:, :], AF.Silu))
                        transpose_block(w2b, b, NE, w2t, ld2)
                        nc.tensor.wait_ge(t_wg.h, t_wg.v)
                        if gi >= 1:
                            nc.tensor.wait_ge(t_ym.h, t_ym.v)
                        for k in range(NE):
                            ins = nc.tensor.matmul(psU[:, :], w2t[:, k, :],
                                                   xn_ap(k, 512 * th, 512),
                                                   start=(k == 0), stop=(k == NE - 1))
                        t_mm.inc(ins)
                        nc.vector.wait_ge(t_mm.h, t_mm.v)
                        nc.vector.wait_ge(t_si.h, t_si.v)
                        if slab >= 1 and icl == 0:
                            nc.vector.wait_ge(t_dn.h, t_dn.v)
                        t_ym.inc(nc.vector.tensor_tensor(Y_sb[:, icl, :], psU[:, :],
                                                         gs_sb[:, b, :], ALU.mult))
                        tw_after_ic[gi] = t_wg.v
                    for hc in range(NE):
                        if hc >= 1 or slab >= 1:
                            nc.gpsimd.wait_ge(t_wg.h, t_wg.v)
                        src3s = bass.AP(w3_in.tensor, 128 * hc * I + 128 * SLAB * slab,
                                        [[I, 128], [128, SLAB], [1, 128]])
                        gd.inc(nc.gpsimd.dma_start(out=w3b[:, :, :], in_=src3s), 16)
                        transpose_block(w3b, None, SLAB, w3t, gd.v)
                        bb = hc % 2
                        nc.tensor.wait_ge(t_wg.h, t_wg.v)
                        nc.tensor.wait_ge(t_ym.h, t_ym.v)
                        nc.tensor.wait_ge(t_ad.h, max(0, t_ad.v - 1))
                        for icl in range(SLAB):
                            ins = nc.tensor.matmul(psF8[bb][:, :], w3t[:, icl, :],
                                                   Y_sb[:, icl, :],
                                                   start=(icl == 0), stop=(icl == SLAB - 1))
                        t_dn.inc(ins)
                        nc.vector.wait_ge(t_dn.h, t_dn.v)
                        t_ad.inc(nc.vector.tensor_tensor(
                            bass.AP(OutAcc, hc * 512, [[NE * 512, 128], [1, 512]]),
                            psF8[bb][:, :],
                            bass.AP(OutAcc, hc * 512, [[NE * 512, 128], [1, 512]]),
                            ALU.add))
                nc.all_engine_barrier()
                if "d_acc2" in dbg:
                    gd.inc(nc.gpsimd.dma_start(
                        out=bass.AP(dbg["d_acc2"].tensor, th * NE * 512,
                                    [[2 * NE * 512, 128], [1, NE * 512]]),
                        in_=bass.AP(OutAcc, 0, [[NE * 512, 128], [1, NE * 512]])), 16)
                    nc.gpsimd.wait_ge(gd.h, gd.v)
                    nc.all_engine_barrier()
            with ExitStack() as p9:
                xt9 = p9.enter_context(nc.sbuf_tensor(f"xt9{th}", [128, 2, E], F32))
                stb = p9.enter_context(nc.sbuf_tensor(f"stb{th}", [128, 2, E], F32))
                psT9 = [p9.enter_context(nc.psum_tensor(f"psT9{th}_{i}", [128, 1024], FP16))
                        for i in range(2)]
                bi9 = 0
                store_after = {}
                nc.tensor.wait_ge(t_ad.h, t_ad.v)
                for tt in range(4):
                    b = tt % 2
                    row0 = 512 * th + 128 * tt
                    if tt >= 2:
                        nc.sync.wait_ge(ev.h, ev.v)
                    sd.inc(nc.sync.dma_start(out=xt9[:, b, :],
                                             in_=x_in[row0:row0 + 128, :]), 16)
                    ldv = sd.v
                    for g in range(4):
                        psb = psT9[bi9 % 2]
                        if bi9 >= 2:
                            nc.tensor.wait_ge(ev.h, ev.v - 1)
                        for h8 in range(8):
                            hc = 8 * g + h8
                            ins = nc.tensor.transpose(
                                psb[:, 128 * h8:128 * h8 + 128],
                                bass.AP(OutAcc, hc * 512 + 128 * tt,
                                        [[NE * 512, 128], [1, 128]]),
                                identh[:, :])
                        pt.inc(ins)
                        nc.vector.wait_ge(pt.h, pt.v)
                        nc.vector.wait_ge(sd.h, ldv)
                        if g == 0 and tt >= 2:
                            nc.vector.wait_ge(t_st.h, store_after[tt - 2])
                        ev.inc(nc.vector.tensor_tensor(
                            stb[:, b, 1024 * g:1024 * g + 1024],
                            bass.AP(psb, 0, [[1024, 128], [1, 1024]]),
                            xt9[:, b, 1024 * g:1024 * g + 1024], ALU.add))
                        bi9 += 1
                    nc.sync.wait_ge(ev.h, ev.v)
                    t_st.inc(nc.sync.dma_start(out=out[row0:row0 + 128, :],
                                               in_=stb[:, b, :]), 16)
                    store_after[tt] = t_st.v
                nc.sync.wait_ge(t_st.h, t_st.v)
                nc.all_engine_barrier()

    nc.sync.wait_ge(t_st.h, t_st.v)
    big.close()
    return nc


def kernel(L0_attn_hidden_states, L0_attn_w_qkv, L0_attn_w_c, L0_ffn_ln_w,
           L0_ffn_gate_w, L0_ffn_up_w, L0_ffn_down_w):
    x = np.ascontiguousarray(np.asarray(L0_attn_hidden_states, dtype=np.float32)[0])
    wqkv = np.ascontiguousarray(np.asarray(L0_attn_w_qkv, dtype=np.float32))
    wc = np.ascontiguousarray(np.asarray(L0_attn_w_c, dtype=np.float32))
    lnw = np.ascontiguousarray(np.asarray(L0_ffn_ln_w, dtype=np.float32))
    w1 = np.ascontiguousarray(np.asarray(L0_ffn_gate_w, dtype=np.float32))
    w2 = np.ascontiguousarray(np.asarray(L0_ffn_up_w, dtype=np.float32))
    w3 = np.ascontiguousarray(np.asarray(L0_ffn_down_w, dtype=np.float32))

    nc = build_nc()
    in_maps = []
    for c in range(NCORE):
        in_maps.append({
            "x_own": np.ascontiguousarray(x[TOK * c:TOK * (c + 1)]),
            "w_qkv": wqkv, "w_c": wc, "ln_w": lnw,
            "gate_w": w1, "up_w": w2, "down_w": w3,
            "hids": np.array([[c, c + 8, c + 16, c + 24]], dtype=np.float32),
        })
    res = run_bass_kernel_spmd(nc, in_maps, core_ids=list(range(NCORE)))
    out = np.concatenate([np.asarray(res.results[c]["out"]) for c in range(NCORE)], axis=0)
    return out.reshape(1, S, E).astype(np.float32)


# version 06
# speedup vs baseline: 1.0177x; 1.0177x over previous
import math
from contextlib import ExitStack

import numpy as np
import ml_dtypes
import concourse.bass as bass
import concourse.mybir as mybir
from concourse.bass_utils import run_bass_kernel_spmd
from concourse.aot_env import allow_ici_all_to_all

F32 = mybir.dt.float32
BF16 = mybir.dt.bfloat16
FP16 = mybir.dt.float16
AF = mybir.ActivationFunctionType
ALU = mybir.AluOpType
AX = mybir.AxisListType

NCORE = 8
S = 8192
E = 4096
H = 32
D = 128
I = 14336
TOK = S // NCORE
NE = E // 128
NI = I // 128
NOC = (E + 2 * D) // 128
QSCALE = float(D) ** -0.5
EPS = 1e-5
LN2_4 = math.log(2.0) / 4.0
SKIP_THRESH = 150.0
SLAB = 28
NSLAB = NI // SLAB

DEBUG = {}


def chunks_for(qt, l):
    i0 = 128 * qt
    slope_min = 2.0 ** (-(8 * l + 8) / 4.0)
    cut = SKIP_THRESH / slope_min
    c_lo = max(0, int(math.ceil((i0 - 511 - cut) / 512.0)))
    return list(range(c_lo, i0 // 512 + 1))


class Ctr:
    __slots__ = ("h", "v")

    def __init__(self, nc, stack, name):
        self.h = stack.enter_context(nc.semaphore(name))
        self.v = 0

    def inc(self, inst, n=1):
        inst.then_inc(self.h, n)
        self.v += n
        return self.v

    def wait(self, eng, v=None):
        v = self.v if v is None else v
        if v > 0:
            eng.wait_ge(self.h, v)


def build_nc():
    nc = bass.Bass("TRN2", target_bir_lowering=False)

    x_in = nc.dram_tensor("x_own", [TOK, E], F32, kind="ExternalInput").ap()
    wqkv = nc.dram_tensor("w_qkv", [E, E + 2 * D], F32, kind="ExternalInput").ap()
    wc_in = nc.dram_tensor("w_c", [E, E], F32, kind="ExternalInput").ap()
    lnw_in = nc.dram_tensor("ln_w", [E], F32, kind="ExternalInput").ap()
    w1_in = nc.dram_tensor("gate_w", [I, E], F32, kind="ExternalInput").ap()
    w2_in = nc.dram_tensor("up_w", [I, E], F32, kind="ExternalInput").ap()
    w3_in = nc.dram_tensor("down_w", [E, I], F32, kind="ExternalInput").ap()
    hids = nc.dram_tensor("hids", [1, 4], F32, kind="ExternalInput").ap()
    out = nc.dram_tensor("out", [TOK, E], F32, kind="ExternalOutput").ap()

    kv_loc = nc.dram_tensor("kv_loc", [TOK, 2 * D], BF16).ap()
    kv_all = nc.dram_tensor("kv_all", [S, 2 * D], BF16).ap()
    q_loc = nc.dram_tensor("q_loc", [E, TOK], BF16).ap()
    q_all = nc.dram_tensor("q_all", [E, TOK], BF16).ap()
    o_loc = nc.dram_tensor("o_loc", [S, 4 * D], BF16).ap()
    attn_all = nc.dram_tensor("attn_all", [S, 4 * D], BF16).ap()
    s_dram = nc.dram_tensor("s_dram", [1, TOK], F32).ap()

    dbg = {}
    for name, shape, dt in DEBUG.get("outs", []):
        dbg[name] = nc.dram_tensor(name, shape, dt, kind="ExternalOutput").ap()

    ident_b = nc.inline_tensor(np.eye(128, dtype=ml_dtypes.bfloat16), "c_identb").ap()
    ident_h = nc.inline_tensor(np.eye(128, dtype=np.float16), "c_identh").ap()
    ident_f = nc.inline_tensor(np.eye(128, dtype=np.float32), "c_identf").ap()
    iota_c = nc.inline_tensor(np.arange(128, dtype=np.float32)[:, None], "c_iota").ap()
    mt_np = (np.arange(128)[:, None] <= np.arange(128)[None, :])
    maskT_c = nc.inline_tensor(mt_np.astype(ml_dtypes.bfloat16), "c_maskT").ap()
    ml_np = (np.arange(128)[:, None] >= np.arange(128)[None, :])
    maskL_c = nc.inline_tensor(ml_np.astype(ml_dtypes.bfloat16), "c_maskL").ap()
    onesr_c = nc.inline_tensor(np.ones((1, 128), dtype=np.float32), "c_onesr").ap()
    onescb_c = nc.inline_tensor(np.ones((128, 1), dtype=ml_dtypes.bfloat16), "c_onescb").ap()

    big = ExitStack()
    C = lambda name: Ctr(nc, big, name)
    sd = C("sd"); gd = C("gd"); st = C("st"); pt = C("pt"); ev = C("ev"); ccs = C("ccs")
    q_mm = C("q_mm"); q_ev = C("q_ev")
    a_sc = C("a_sc"); a_ex = C("a_ex"); a_tr = C("a_tr"); a_ev = C("a_ev")
    a_av = C("a_av"); a_bv = C("a_bv"); a_rc = C("a_rc"); a_rec = C("a_rec"); a_os = C("a_os"); a_ms = C("a_ms")
    t_tr = C("t_tr"); t_wg = C("t_wg"); t_mm = C("t_mm"); t_si = C("t_si")
    t_ym = C("t_ym"); t_dn = C("t_dn"); t_ad = C("t_ad"); t_st = C("t_st")

    sb = lambda name, shape, dt: big.enter_context(nc.sbuf_tensor(name, shape, dt))
    identb = sb("identb", [128, 128], BF16)
    identh = sb("identh", [128, 128], FP16)
    identf = sb("identf", [128, 128], F32)
    iotai = sb("iotai", [128, 1], F32)
    maskT = sb("maskT", [128, 128], BF16)
    maskL = sb("maskL", [128, 128], BF16)
    onesr = sb("onesr", [1, 128], F32)
    onescb = sb("onescb", [128, 1], BF16)
    lnw_sb = sb("lnw_sb", [128, NE], F32)
    slope_bc = sb("slope_bc", [128, 4], F32)
    nslope_bc = sb("nslope_bc", [128, 4], F32)
    wvec = sb("wvec", [128, 16], F32)
    s_bc = sb("s_bc", [128, TOK], F32)
    s_cols = sb("s_cols", [128, 8], F32)
    cvals = sb("cvals", [128, 2], F32)
    xn_fm = sb("xn_fm", [128, NE, TOK], BF16)

    def xn_ap(k, c0, n):
        return bass.AP(xn_fm, k * TOK + c0, [[NE * TOK, 128], [1, n]])

    def ecopy(eng, out_ap, in_ap):
        return nc.scalar.copy(out_ap, in_ap) if eng is nc.scalar else eng.tensor_copy(out_ap, in_ap)

    sd.inc(nc.sync.dma_start(out=identb[:, :], in_=ident_b), 16)
    sd.inc(nc.sync.dma_start(out=identh[:, :], in_=ident_h), 16)
    sd.inc(nc.sync.dma_start(out=identf[:, :], in_=ident_f), 16)
    sd.inc(nc.sync.dma_start(out=iotai[:, :], in_=iota_c), 16)
    sd.inc(nc.sync.dma_start(out=maskT[:, :], in_=maskT_c), 16)
    sd.inc(nc.sync.dma_start(out=maskL[:, :], in_=maskL_c), 16)
    sd.inc(nc.sync.dma_start(out=onesr[:, :], in_=onesr_c), 16)
    sd.inc(nc.sync.dma_start(out=onescb[:, :], in_=onescb_c), 16)
    gd.inc(nc.gpsimd.dma_start(out=lnw_sb[:, :],
                               in_=bass.AP(lnw_in.tensor, 0, [[1, 128], [128, NE]]),
                               allow_slow_non_contiguous=True), 16)

    with nc.psum_tensor("ps0", [128, 512], F32) as ps0, \
         nc.sbuf_tensor("hrow", [1, 4], F32) as hrow, \
         nc.sbuf_tensor("srow4", [1, 4], F32) as srow4, \
         nc.sbuf_tensor("wtmp", [128, 1], F32) as wtmp:
        sd.inc(nc.sync.dma_start(out=hrow[:, :], in_=hids), 16)
        st.inc(nc.vector.memset(cvals[:, 0:1], -LN2_4))
        st.wait(nc.vector)
        st.inc(nc.vector.memset(cvals[:, 1:2], EPS))
        sd.wait(nc.scalar); gd.wait(nc.scalar); st.wait(nc.scalar)
        st.inc(nc.scalar.activation(srow4[:, :], hrow[:, :], AF.Exp,
                                    bias=cvals[0:1, 0:1], scale=-LN2_4))
        st.wait(nc.tensor); sd.wait(nc.tensor)
        st.inc(nc.tensor.matmul(ps0[:, 0:4], onesr[0:1, :], srow4[:, :],
                                start=True, stop=True))
        st.wait(nc.vector)
        st.inc(nc.vector.tensor_copy(slope_bc[:, :], ps0[:, 0:4]))
        st.wait(nc.vector)
        st.inc(nc.vector.tensor_scalar_mul(nslope_bc[:, :], slope_bc[:, :], -1.0))
        for l in range(4):
            for s_ in range(4):
                off = -64.0 if l == 0 else float(128 * s_ - 256)
                st.wait(nc.vector)
                st.inc(nc.vector.tensor_scalar(wtmp[:, :], iotai[:, :], off,
                                               slope_bc[:, l:l + 1], ALU.add, ALU.mult))
                st.wait(nc.scalar)
                st.inc(nc.scalar.activation(wvec[:, 4 * l + s_:4 * l + s_ + 1],
                                            wtmp[:, :], AF.Exp))
        nc.all_engine_barrier()

        with nc.sbuf_tensor("xt", [128, 2, E], F32) as xt, \
             nc.sbuf_tensor("sqscr", [128, E], BF16) as sqscr, \
             nc.sbuf_tensor("stat", [128, 10], F32) as stat, \
             nc.sbuf_tensor("xnt", [128, E], BF16) as xnt, \
             nc.sbuf_tensor("s8", [8, 128], F32) as s8, \
             nc.sbuf_tensor("srowT", [1, TOK], F32) as srowT, \
             nc.psum_tensor("psT1a", [128, 1024], BF16) as psT1a, \
             nc.psum_tensor("psT1b", [128, 1024], BF16) as psT1b:
            c_sum = stat[:, 0:1]; c_ssq = stat[:, 1:2]; c_m = stat[:, 2:3]
            c_e2 = stat[:, 3:4]; c_m2 = stat[:, 4:5]; c_var = stat[:, 5:6]
            c_ln = stat[:, 6:7]; c_r = stat[:, 7:8]; c_rr = stat[:, 8:9]
            c_mx = stat[:, 9:10]
            xn_done_st = [0] * 8
            bi = 0
            for tt in range(8):
                b = tt % 2
                if tt >= 2:
                    nc.sync.wait_ge(st.h, xn_done_st[tt - 2])
                sd.inc(nc.sync.dma_start(out=xt[:, b, :],
                                         in_=x_in[128 * tt:128 * tt + 128, :]), 16)
                nc.vector.wait_ge(sd.h, sd.v)
                nc.scalar.wait_ge(sd.h, sd.v)
                st.inc(nc.vector.reduce_sum(c_sum, xt[:, b, :], AX.X))
                st.wait(nc.scalar)
                st.inc(nc.scalar.activation(sqscr[:, :], xt[:, b, :], AF.Square,
                                            accum_out=c_ssq))
                st.wait(nc.vector)
                st.inc(nc.vector.tensor_scalar_mul(c_m, c_sum, 1.0 / E))
                st.wait(nc.vector)
                st.inc(nc.vector.tensor_scalar_mul(c_e2, c_ssq, 1.0 / E))
                st.wait(nc.vector)
                st.inc(nc.vector.tensor_mul(c_m2, c_m, c_m))
                st.wait(nc.vector)
                st.inc(nc.vector.tensor_sub(c_var, c_e2, c_m2))
                st.wait(nc.scalar)
                st.inc(nc.scalar.activation(c_ln, c_var, AF.Ln, bias=cvals[:, 1:2]))
                st.wait(nc.scalar)
                st.inc(nc.scalar.activation(c_r, c_ln, AF.Exp, scale=-0.5))
                st.wait(nc.scalar)
                st.inc(nc.scalar.activation(c_rr, c_ln, AF.Exp, scale=-1.0))
                st.wait(nc.vector)
                st.inc(nc.vector.tensor_mul(c_mx, c_var, c_rr))
                st.wait(nc.scalar)
                st.inc(nc.scalar.activation(c_ln, c_mx, AF.Ln, bias=cvals[:, 1:2]))
                st.wait(nc.scalar)
                st.inc(nc.scalar.activation(s_cols[:, tt:tt + 1], c_ln, AF.Exp, scale=-0.5))
                st.wait(nc.vector)
                if tt >= 1:
                    nc.vector.wait_ge(pt.h, 4 * tt)
                st.inc(nc.vector.tensor_scalar(xnt[:, :], xt[:, b, :], c_m, c_r,
                                               ALU.subtract, ALU.mult))
                xn_done_st[tt] = st.v
                st.wait(nc.tensor)
                for g in range(4):
                    psb = psT1a if bi % 2 == 0 else psT1b
                    if bi >= 2:
                        nc.tensor.wait_ge(ev.h, bi - 1)
                    for k8 in range(8):
                        k = 8 * g + k8
                        ins = nc.tensor.transpose(psb[:, 128 * k8:128 * k8 + 128],
                                                  xnt[:, 128 * k:128 * k + 128],
                                                  identb[:, :])
                    pt.inc(ins)
                    eng = nc.vector if bi % 2 == 0 else nc.scalar
                    eng.wait_ge(pt.h, pt.v)
                    dst = bass.AP(xn_fm, TOK * 8 * g + 128 * tt,
                                  [[NE * TOK, 128], [TOK, 8], [1, 128]])
                    ev.inc(ecopy(eng,
                        dst, bass.AP(psb, 0, [[1024, 128], [128, 8], [1, 128]])))
                    bi += 1
            nc.tensor.wait_ge(st.h, st.v)
            st.inc(nc.tensor.transpose(ps0[0:8, 0:128], s_cols[:, :], identf[:, :]))
            st.wait(nc.vector)
            st.inc(nc.vector.tensor_copy(s8[:, :], ps0[0:8, 0:128]))
            st.wait(nc.sync)
            sd.inc(nc.sync.dma_start(out=bass.AP(s_dram.tensor, 0, [[128, 8], [1, 128]]),
                                     in_=s8[:, :]), 16)
            nc.sync.wait_ge(sd.h, sd.v)
            sd.inc(nc.sync.dma_start(out=srowT[:, :], in_=s_dram), 16)
            nc.tensor.wait_ge(sd.h, sd.v)
            for t2 in range(2):
                if t2 == 1:
                    nc.tensor.wait_ge(st.h, st.v)
                st.inc(nc.tensor.matmul(ps0[:, :], onesr[0:1, :],
                                        srowT[:, 512 * t2:512 * t2 + 512],
                                        start=True, stop=True))
                st.wait(nc.vector)
                st.inc(nc.vector.tensor_copy(s_bc[:, 512 * t2:512 * t2 + 512], ps0[:, :]))
            nc.all_engine_barrier()
            if "d_xn" in dbg:
                gd.inc(nc.gpsimd.dma_start(
                    out=dbg["d_xn"],
                    in_=bass.AP(xn_fm, 0, [[NE * TOK, 128], [1, NE * TOK]])), 16)
                nc.gpsimd.wait_ge(gd.h, gd.v)
                nc.all_engine_barrier()

    with ExitStack() as p2:
        wcb = p2.enter_context(nc.sbuf_tensor("wcb", [128, 2, NE, 128], BF16))
        qstg = p2.enter_context(nc.sbuf_tensor("qstg", [128, 2, 512], BF16))
        kvfm = p2.enter_context(nc.sbuf_tensor("kvfm", [128, 2, TOK], BF16))
        kvtok = p2.enter_context(nc.sbuf_tensor("kvtok", [128, 8, 256], BF16))
        psQ = [p2.enter_context(nc.psum_tensor(f"psQ{i}", [128, 512], F32)) for i in range(2)]
        psP2 = [p2.enter_context(nc.psum_tensor(f"psP2{i}", [128, 1024], BF16)) for i in range(2)]

        q_store_sd = {}
        nc.tensor.wait_ge(ev.h, ev.v)
        nc.tensor.wait_ge(st.h, st.v)
        for oc in range(NOC):
            b = oc % 2
            if oc >= 2:
                nc.gpsimd.wait_ge(q_mm.h, 2 * (oc - 2) + 2)
            for qq in range(4):
                src = bass.AP(wqkv.tensor, 128 * oc + (E + 2 * D) * 128 * 8 * qq,
                              [[E + 2 * D, 128], [(E + 2 * D) * 128, 8], [1, 128]])
                gd.inc(nc.gpsimd.dma_start(out=wcb[:, b, 8 * qq:8 * qq + 8, :], in_=src), 16)
            for t in range(2):
                u2 = 2 * oc + t
                nc.tensor.wait_ge(gd.h, gd.v)
                if u2 >= 2:
                    nc.tensor.wait_ge(q_ev.h, u2 - 1)
                for k in range(NE):
                    ins = nc.tensor.matmul(psQ[t][:, :], wcb[:, b, k, :],
                                           xn_ap(k, 512 * t, 512),
                                           start=(k == 0), stop=(k == NE - 1))
                q_mm.inc(ins)
                nc.vector.wait_ge(q_mm.h, u2 + 1)
                if oc < 32:
                    if u2 >= 2 and (u2 - 2) in q_store_sd:
                        nc.vector.wait_ge(sd.h, q_store_sd[u2 - 2])
                    q_ev.inc(nc.vector.tensor_scalar_mul(qstg[:, u2 % 2, :],
                                                         psQ[t][:, :], QSCALE))
                    nc.sync.wait_ge(q_ev.h, u2 + 1)
                    row0 = 512 * (oc % 8) + 128 * (oc // 8)
                    sd.inc(nc.sync.dma_start(
                        out=bass.AP(q_loc.tensor, row0 * TOK + 512 * t,
                                    [[TOK, 128], [1, 512]]),
                        in_=qstg[:, u2 % 2, :]), 16)
                    q_store_sd[u2] = sd.v
                else:
                    q_ev.inc(nc.vector.tensor_copy(
                        bass.AP(kvfm, (oc - 32) * TOK + 512 * t, [[2 * TOK, 128], [1, 512]]),
                        psQ[t][:, :]))
        nc.all_engine_barrier()
        nc.tensor.wait_ge(q_ev.h, q_ev.v)
        for tt in range(8):
            bb = tt % 2
            psb = psP2[bb]
            if tt >= 2:
                nc.tensor.wait_ge(ev.h, ev.v - 1)
            nc.tensor.transpose(psb[:, 0:128],
                                bass.AP(kvfm, 128 * tt, [[2 * TOK, 128], [1, 128]]),
                                identb[:, :])
            ins = nc.tensor.transpose(psb[:, 128:256],
                                      bass.AP(kvfm, TOK + 128 * tt, [[2 * TOK, 128], [1, 128]]),
                                      identb[:, :])
            pt.inc(ins)
            nc.vector.wait_ge(pt.h, pt.v)
            ev.inc(nc.vector.tensor_copy(kvtok[:, tt, :], psb[:, 0:256]))
        nc.sync.wait_ge(ev.h, ev.v)
        sd.inc(nc.sync.dma_start(
            out=bass.AP(kv_loc.tensor, 0, [[256, 128], [256 * 128, 8], [1, 256]]),
            in_=kvtok[:, :, :]), 16)
        nc.sync.wait_ge(sd.h, sd.v)
        nc.all_engine_barrier()

    ccs.inc(nc.gpsimd.collective_compute(
        "AllGather", ALU.bypass, replica_groups=[list(range(NCORE))],
        ins=[kv_loc.opt()], outs=[kv_all.opt()]))
    with allow_ici_all_to_all():
        ccs.inc(nc.gpsimd.collective_compute(
            "AllToAll", ALU.bypass, replica_groups=[list(range(NCORE))],
            ins=[q_loc.opt()], outs=[q_all.opt()]))
    nc.gpsimd.wait_ge(ccs.h, ccs.v)
    nc.all_engine_barrier()
    if "d_kv" in dbg:
        gd.inc(nc.gpsimd.dma_start(out=dbg["d_kv"], in_=kv_all), 16)
        nc.gpsimd.wait_ge(gd.h, gd.v)
    if "d_q" in dbg:
        gd.inc(nc.gpsimd.dma_start(out=dbg["d_q"], in_=q_all), 16)
        nc.gpsimd.wait_ge(gd.h, gd.v)

    with ExitStack() as p4:
        qg = p4.enter_context(nc.sbuf_tensor("qg", [128, 4, 8, TOK], BF16))
        K_sb = p4.enter_context(nc.sbuf_tensor("K_sb", [128, S], BF16))
        V_sb = p4.enter_context(nc.sbuf_tensor("V_sb", [128, 64, 128], BF16))
        prob = p4.enter_context(nc.sbuf_tensor("prob", [128, 2, 512], BF16))
        ptb = p4.enter_context(nc.sbuf_tensor("ptb", [128, 2, 512], BF16))
        bvec = p4.enter_context(nc.sbuf_tensor("bvec", [128, 2, 4], F32))
        recb = p4.enter_context(nc.sbuf_tensor("recb", [128, 2], F32))
        ostg = p4.enter_context(nc.sbuf_tensor("ostg", [128, 2, 512], BF16))
        psS = [p4.enter_context(nc.psum_tensor(f"psS{i}", [128, 512], F32)) for i in range(2)]
        psPT = [p4.enter_context(nc.psum_tensor(f"psPT{i}", [128, 1024], BF16)) for i in range(2)]
        psO = [p4.enter_context(nc.psum_tensor(f"psO{i}", [128, 512], F32)) for i in range(2)]
        psD = [p4.enter_context(nc.psum_tensor(f"psD{i}", [128, 512], F32)) for i in range(2)]

        sd.inc(nc.sync.dma_start(out=K_sb[:, :], in_=kv_all[:, 0:128], transpose=True), 16)
        sd.inc(nc.sync.dma_start(
            out=V_sb[:, :, :],
            in_=bass.AP(kv_all.tensor, 128, [[256, 128], [256 * 128, 64], [1, 128]])), 16)
        for hl4 in range(4):
            sd.inc(nc.sync.dma_start(
                out=qg[:, hl4, :, :],
                in_=bass.AP(q_all.tensor, 128 * TOK * hl4,
                            [[TOK, 128], [512 * TOK, 8], [1, TOK]])), 16)
        nc.sync.wait_ge(sd.h, sd.v)
        nc.all_engine_barrier()

        units = []
        for qt in range(64):
            for hli in range(4):
                cl = chunks_for(qt, hli)
                for ci, c in enumerate(cl):
                    diag = (c == qt // 4)
                    ns = (qt % 4) + 1 if diag else 4
                    units.append((qt, hli, c, ns, diag, ci == 0, ci == len(cl) - 1))
        U = len(units)
        a_os_after = {}

        for u in range(U + 2):
            if u < U:
                qt, l, c, ns, diag, first, last = units[u]
                i0 = 128 * qt
                nc.tensor.wait_ge(a_ex.h, max(0, u - 1))
                ins = nc.tensor.matmul(psS[u % 2][:, 0:128 * ns],
                                       qg[:, l, qt // 8, 128 * (qt % 8):128 * (qt % 8) + 128],
                                       K_sb[:, 512 * c:512 * c + 128 * ns],
                                       start=True, stop=True)
                a_sc.inc(ins)
                nc.vector.wait_ge(a_ex.h, max(0, u - 1))
                if l == 0:
                    for s_ in range(ns):
                        jc = 512 * c + 128 * s_ + 64
                        ins = nc.vector.tensor_scalar(bvec[:, u % 2, s_:s_ + 1], iotai[:, :],
                                                      float(i0 - jc), nslope_bc[:, 0:1],
                                                      ALU.add, ALU.mult)
                else:
                    jc = 512 * c + 256
                    ins = nc.vector.tensor_scalar(bvec[:, u % 2, 0:1], iotai[:, :],
                                                  float(i0 - jc), nslope_bc[:, l:l + 1],
                                                  ALU.add, ALU.mult)
                a_bv.inc(ins)
                nc.scalar.wait_ge(a_sc.h, u + 1)
                nc.scalar.wait_ge(a_bv.h, u + 1)
                nc.scalar.wait_ge(a_tr.h, max(0, u - 1))
                if l == 0:
                    for s_ in range(ns):
                        ins = nc.scalar.activation(prob[:, u % 2, 128 * s_:128 * s_ + 128],
                                                   psS[u % 2][:, 128 * s_:128 * s_ + 128],
                                                   AF.Exp, bias=bvec[:, u % 2, s_:s_ + 1])
                else:
                    ins = nc.scalar.activation(prob[:, u % 2, 0:128 * ns],
                                               psS[u % 2][:, 0:128 * ns],
                                               AF.Exp, bias=bvec[:, u % 2, 0:1])
                a_ex.inc(ins)
            if 1 <= u <= U:
                v = u - 1
                qt, l, c, ns, diag, first, last = units[v]
                e = 4 * qt + l
                if diag:
                    nc.vector.wait_ge(a_ex.h, v + 1)
                    sdiag = ns - 1
                    a_ms.inc(nc.vector.tensor_mul(
                        prob[:, v % 2, 128 * sdiag:128 * sdiag + 128],
                        prob[:, v % 2, 128 * sdiag:128 * sdiag + 128],
                        maskL[:, :]))
                nc.tensor.wait_ge(a_ex.h, v + 1)
                if diag:
                    nc.tensor.wait_ge(a_ms.h, a_ms.v)
                nc.tensor.wait_ge(a_ev.h, max(0, v - 1))
                for s_ in range(ns):
                    ins = nc.tensor.transpose(psPT[v % 2][:, 128 * s_:128 * s_ + 128],
                                              prob[:, v % 2, 128 * s_:128 * s_ + 128],
                                              identb[:, :])
                a_tr.inc(ins)
                nc.vector.wait_ge(a_tr.h, v + 1)
                nc.vector.wait_ge(a_av.h, max(0, v - 1))
                for s_ in range(ns):
                    wcol = wvec[:, 4 * l:4 * l + 1] if l == 0 else wvec[:, 4 * l + s_:4 * l + s_ + 1]
                    dstp = ptb[:, v % 2, 128 * s_:128 * s_ + 128]
                    srcp = psPT[v % 2][:, 128 * s_:128 * s_ + 128]
                    ins = nc.vector.tensor_scalar(dstp, srcp, wcol, None, ALU.mult)
                a_ev.inc(ins)
            if u >= 2:
                v2 = u - 2
                qt, l, c, ns, diag, first, last = units[v2]
                e = 4 * qt + l
                if first:
                    nc.tensor.wait_ge(a_rec.h, max(0, e - 1))
                nc.tensor.wait_ge(a_ev.h, v2 + 1)
                for s_ in range(ns):
                    jt = 4 * c + s_
                    nc.tensor.matmul(psO[e % 2][:, 0:128],
                                     ptb[:, v2 % 2, 128 * s_:128 * s_ + 128],
                                     V_sb[:, jt, :],
                                     start=(first and s_ == 0), stop=(last and s_ == ns - 1))
                for s_ in range(ns):
                    ins = nc.tensor.matmul(psD[e % 2][:, 0:1],
                                           ptb[:, v2 % 2, 128 * s_:128 * s_ + 128],
                                           onescb[:, :],
                                           start=(first and s_ == 0), stop=(last and s_ == ns - 1))
                a_av.inc(ins)
                if last:
                    nc.vector.wait_ge(a_av.h, v2 + 1)
                    a_rc.inc(nc.vector.reciprocal(recb[:, e % 2:e % 2 + 1],
                                                  psD[e % 2][:, 0:1]))
                    nc.vector.wait_ge(a_rc.h, e + 1)
                    if l == 0 and qt >= 2:
                        nc.vector.wait_ge(a_os.h, a_os_after[qt - 2])
                    a_rec.inc(nc.vector.tensor_scalar(
                        ostg[:, qt % 2, 128 * l:128 * l + 128], psO[e % 2][:, 0:128],
                        recb[:, e % 2:e % 2 + 1], None, ALU.mult))
                    if l == 3:
                        nc.sync.wait_ge(a_rec.h, e + 1)
                        a_os.inc(nc.sync.dma_start(
                            out=bass.AP(o_loc.tensor, 128 * qt * 512, [[512, 128], [1, 512]]),
                            in_=ostg[:, qt % 2, :]), 16)
                        a_os_after[qt] = a_os.v
        nc.sync.wait_ge(a_os.h, a_os.v)
        nc.all_engine_barrier()

    with allow_ici_all_to_all():
        ccs.inc(nc.gpsimd.collective_compute(
            "AllToAll", ALU.bypass, replica_groups=[list(range(NCORE))],
            ins=[o_loc.opt()], outs=[attn_all.opt()]))
    nc.gpsimd.wait_ge(ccs.h, ccs.v)
    nc.all_engine_barrier()
    if "d_attn" in dbg:
        gd.inc(nc.gpsimd.dma_start(out=dbg["d_attn"], in_=attn_all), 16)
        nc.gpsimd.wait_ge(gd.h, gd.v)

    for k in range(NE):
        st.inc(nc.vector.scalar_tensor_tensor(
            bass.AP(xn_fm, k * TOK, [[NE * TOK, 128], [1, TOK]]),
            s_bc[:, :], lnw_sb[:, k:k + 1],
            bass.AP(xn_fm, k * TOK, [[NE * TOK, 128], [1, TOK]]),
            ALU.mult, ALU.mult))
    nc.all_engine_barrier()

    for th in range(2):
        with ExitStack() as pt6:
            OutAcc = pt6.enter_context(nc.sbuf_tensor(f"OutAcc{th}", [128, NE, 512], FP16))
            with ExitStack() as p6a:
                att_ld = p6a.enter_context(nc.sbuf_tensor(f"att_ld{th}", [128, 2, 8, 4, 128], BF16))
                attn_sb = p6a.enter_context(nc.sbuf_tensor(f"attn_sb{th}", [128, NE, 512], BF16))
                psT6 = [p6a.enter_context(nc.psum_tensor(f"psT6{th}_{i}", [128, 1024], BF16))
                        for i in range(2)]
                bi6 = 0
                ld_sd = {}
                for tt in range(4):
                    b = tt % 2
                    if tt >= 2:
                        nc.sync.wait_ge(pt.h, pt.v)
                    for hl4 in range(4):
                        sd.inc(nc.sync.dma_start(
                            out=att_ld[:, b, :, hl4, :],
                            in_=bass.AP(attn_all.tensor, (512 * th + 128 * tt) * 512 + 128 * hl4,
                                        [[512, 128], [1024 * 512, 8], [1, 128]])), 16)
                    ld_sd[tt] = sd.v
                    nc.tensor.wait_ge(sd.h, ld_sd[tt])
                    for g in range(4):
                        psb = psT6[bi6 % 2]
                        if bi6 >= 2:
                            nc.tensor.wait_ge(ev.h, ev.v - 1)
                        for h8 in range(8):
                            h = 8 * g + h8
                            ins = nc.tensor.transpose(
                                psb[:, 128 * h8:128 * h8 + 128],
                                att_ld[:, b, h % 8, h // 8, :], identb[:, :])
                        pt.inc(ins)
                        eng = nc.vector if bi6 % 2 == 0 else nc.scalar
                        eng.wait_ge(pt.h, pt.v)
                        dst = bass.AP(attn_sb, 512 * 8 * g + 128 * tt,
                                      [[NE * 512, 128], [512, 8], [1, 128]])
                        ev.inc(ecopy(eng,
                            dst, bass.AP(psb, 0, [[1024, 128], [128, 8], [1, 128]])))
                        bi6 += 1
                nc.all_engine_barrier()
                wcb2 = p6a.enter_context(nc.sbuf_tensor(f"wcb2{th}", [128, 2, NE, 128], BF16))
                psF6 = [p6a.enter_context(nc.psum_tensor(f"psF6{th}_{i}", [128, 512], F32))
                        for i in range(2)]
                nc.tensor.wait_ge(ev.h, ev.v)
                wc_ld = {}
                for pc in range(2):
                    for qq in range(4):
                        srcp2 = bass.AP(wc_in.tensor, 128 * pc + E * 128 * 8 * qq,
                                        [[E, 128], [E * 128, 8], [1, 128]])
                        gd.inc(nc.gpsimd.dma_start(out=wcb2[:, pc, 8 * qq:8 * qq + 8, :],
                                                   in_=srcp2), 16)
                    wc_ld[pc] = gd.v
                for oc in range(NE):
                    b = oc % 2
                    nc.tensor.wait_ge(gd.h, wc_ld[oc])
                    if oc >= 2:
                        nc.tensor.wait_ge(t_ad.h, t_ad.v - 1)
                    for hh in range(NE):
                        ins = nc.tensor.matmul(psF6[b][:, :], wcb2[:, b, hh, :],
                                               bass.AP(attn_sb, hh * 512,
                                                       [[NE * 512, 128], [1, 512]]),
                                               start=(hh == 0), stop=(hh == NE - 1))
                    t_mm.inc(ins)
                    if oc + 2 < NE:
                        nc.gpsimd.wait_ge(t_mm.h, t_mm.v)
                        for qq in range(4):
                            srcp2 = bass.AP(wc_in.tensor, 128 * (oc + 2) + E * 128 * 8 * qq,
                                            [[E, 128], [E * 128, 8], [1, 128]])
                            gd.inc(nc.gpsimd.dma_start(out=wcb2[:, b, 8 * qq:8 * qq + 8, :],
                                                       in_=srcp2), 16)
                        wc_ld[oc + 2] = gd.v
                    nc.vector.wait_ge(t_mm.h, t_mm.v)
                    t_ad.inc(nc.vector.tensor_copy(
                        bass.AP(OutAcc, oc * 512, [[NE * 512, 128], [1, 512]]),
                        psF6[b][:, :]))
                nc.all_engine_barrier()
                if "d_acc1" in dbg:
                    gd.inc(nc.gpsimd.dma_start(
                        out=bass.AP(dbg["d_acc1"].tensor, th * NE * 512,
                                    [[2 * NE * 512, 128], [1, NE * 512]]),
                        in_=bass.AP(OutAcc, 0, [[NE * 512, 128], [1, NE * 512]])), 16)
                    nc.gpsimd.wait_ge(gd.h, gd.v)
                    nc.all_engine_barrier()
            with ExitStack() as p8:
                w1b = p8.enter_context(nc.sbuf_tensor(f"w1b{th}", [128, 2, NE, 128], BF16))
                w2b = p8.enter_context(nc.sbuf_tensor(f"w2b{th}", [128, 2, NE, 128], BF16))
                w1t = p8.enter_context(nc.sbuf_tensor(f"w1t{th}", [128, NE, 128], BF16))
                w2t = p8.enter_context(nc.sbuf_tensor(f"w2t{th}", [128, NE, 128], BF16))
                w3b = p8.enter_context(nc.sbuf_tensor(f"w3b{th}", [128, 2, SLAB, 128], BF16))
                w3t = p8.enter_context(nc.sbuf_tensor(f"w3t{th}", [128, SLAB, 128], BF16))
                gs_sb = p8.enter_context(nc.sbuf_tensor(f"gs_sb{th}", [128, 2, 512], BF16))
                Y_sb = p8.enter_context(nc.sbuf_tensor(f"Y_sb{th}", [128, SLAB, 512], BF16))
                psT8 = [p8.enter_context(nc.psum_tensor(f"psT8{th}_{i}", [128, 1024], BF16))
                        for i in range(2)]
                psG = p8.enter_context(nc.psum_tensor(f"psG{th}", [128, 512], F32))
                psU = p8.enter_context(nc.psum_tensor(f"psU{th}", [128, 512], F32))
                psF8 = [p8.enter_context(nc.psum_tensor(f"psF8{th}_{i}", [128, 512], F32))
                        for i in range(2)]
                bi8 = 0
                tw_after_ic = {}

                def transpose_block(src3, bidx, n, wdst, ld_val):
                    nonlocal bi8
                    nb = (n + 7) // 8
                    for g in range(nb):
                        kk = list(range(8 * g, min(8 * g + 8, n)))
                        psb = psT8[bi8 % 2]
                        if bi8 >= 2:
                            nc.tensor.wait_ge(t_wg.h, t_wg.v - 1)
                        nc.tensor.wait_ge(gd.h, ld_val)
                        for j, k in enumerate(kk):
                            if bidx is None:
                                ins = nc.tensor.transpose(psb[:, 128 * j:128 * j + 128],
                                                          src3[:, k, :], identb[:, :])
                            else:
                                ins = nc.tensor.transpose(psb[:, 128 * j:128 * j + 128],
                                                          src3[:, bidx, k, :], identb[:, :])
                        t_tr.inc(ins)
                        eng = nc.vector if bi8 % 2 == 0 else nc.scalar
                        eng.wait_ge(t_tr.h, t_tr.v)
                        dst = bass.AP(wdst, 128 * 8 * g,
                                      [[wdst.shape[1] * 128, 128], [128, len(kk)], [1, 128]])
                        t_wg.inc(ecopy(eng,
                            dst, bass.AP(psb, 0, [[1024, 128], [128, len(kk)], [1, 128]])))
                        bi8 += 1

                nc.tensor.wait_ge(st.h, st.v)
                nc.tensor.wait_ge(t_ad.h, t_ad.v)
                w12_ld = {}

                def issue_w12(gi2):
                    b2 = gi2 % 2
                    src1 = bass.AP(w1_in.tensor, 128 * gi2 * E,
                                   [[E, 128], [128, NE], [1, 128]])
                    gd.inc(nc.gpsimd.dma_start(out=w1b[:, b2, :, :], in_=src1), 16)
                    l1 = gd.v
                    src2 = bass.AP(w2_in.tensor, 128 * gi2 * E,
                                   [[E, 128], [128, NE], [1, 128]])
                    gd.inc(nc.gpsimd.dma_start(out=w2b[:, b2, :, :], in_=src2), 16)
                    w12_ld[gi2] = (l1, gd.v)

                issue_w12(0)
                for slab in range(NSLAB):
                    for icl in range(SLAB):
                        gi = SLAB * slab + icl
                        b = gi % 2
                        if gi + 1 < NI:
                            if gi >= 1:
                                nc.gpsimd.wait_ge(t_wg.h, tw_after_ic[gi - 1])
                            issue_w12(gi + 1)
                        ld1, ld2 = w12_ld[gi]
                        transpose_block(w1b, b, NE, w1t, ld1)
                        nc.tensor.wait_ge(t_wg.h, t_wg.v)
                        if gi >= 1:
                            nc.tensor.wait_ge(t_si.h, t_si.v)
                        for k in range(NE):
                            ins = nc.tensor.matmul(psG[:, :], w1t[:, k, :],
                                                   xn_ap(k, 512 * th, 512),
                                                   start=(k == 0), stop=(k == NE - 1))
                        t_mm.inc(ins)
                        nc.scalar.wait_ge(t_mm.h, t_mm.v)
                        if gi >= 2:
                            nc.scalar.wait_ge(t_ym.h, t_ym.v - 1)
                        t_si.inc(nc.scalar.activation(gs_sb[:, b, :], psG[:, :], AF.Silu))
                        transpose_block(w2b, b, NE, w2t, ld2)
                        nc.tensor.wait_ge(t_wg.h, t_wg.v)
                        if gi >= 1:
                            nc.tensor.wait_ge(t_ym.h, t_ym.v)
                        for k in range(NE):
                            ins = nc.tensor.matmul(psU[:, :], w2t[:, k, :],
                                                   xn_ap(k, 512 * th, 512),
                                                   start=(k == 0), stop=(k == NE - 1))
                        t_mm.inc(ins)
                        nc.vector.wait_ge(t_mm.h, t_mm.v)
                        nc.vector.wait_ge(t_si.h, t_si.v)
                        if slab >= 1 and icl == 0:
                            nc.vector.wait_ge(t_dn.h, t_dn.v)
                        t_ym.inc(nc.vector.tensor_tensor(Y_sb[:, icl, :], psU[:, :],
                                                         gs_sb[:, b, :], ALU.mult))
                        tw_after_ic[gi] = t_wg.v
                    w3_ld = {}

                    def issue_w3(hc2):
                        src3s = bass.AP(w3_in.tensor, 128 * hc2 * I + 128 * SLAB * slab,
                                        [[I, 128], [128, SLAB], [1, 128]])
                        gd.inc(nc.gpsimd.dma_start(out=w3b[:, hc2 % 2, :, :], in_=src3s), 16)
                        w3_ld[hc2] = gd.v

                    nc.gpsimd.wait_ge(t_wg.h, t_wg.v)
                    issue_w3(0)
                    tw_w3 = {}
                    for hc in range(NE):
                        if hc + 1 < NE:
                            if hc >= 1:
                                nc.gpsimd.wait_ge(t_wg.h, tw_w3[hc - 1])
                            issue_w3(hc + 1)
                        transpose_block(w3b[:, hc % 2], None, SLAB, w3t, w3_ld[hc])
                        tw_w3[hc] = t_wg.v
                        bb = hc % 2
                        nc.tensor.wait_ge(t_wg.h, t_wg.v)
                        nc.tensor.wait_ge(t_ym.h, t_ym.v)
                        nc.tensor.wait_ge(t_ad.h, max(0, t_ad.v - 1))
                        for icl in range(SLAB):
                            ins = nc.tensor.matmul(psF8[bb][:, :], w3t[:, icl, :],
                                                   Y_sb[:, icl, :],
                                                   start=(icl == 0), stop=(icl == SLAB - 1))
                        t_dn.inc(ins)
                        nc.vector.wait_ge(t_dn.h, t_dn.v)
                        t_ad.inc(nc.vector.tensor_tensor(
                            bass.AP(OutAcc, hc * 512, [[NE * 512, 128], [1, 512]]),
                            psF8[bb][:, :],
                            bass.AP(OutAcc, hc * 512, [[NE * 512, 128], [1, 512]]),
                            ALU.add))
                nc.all_engine_barrier()
                if "d_acc2" in dbg:
                    gd.inc(nc.gpsimd.dma_start(
                        out=bass.AP(dbg["d_acc2"].tensor, th * NE * 512,
                                    [[2 * NE * 512, 128], [1, NE * 512]]),
                        in_=bass.AP(OutAcc, 0, [[NE * 512, 128], [1, NE * 512]])), 16)
                    nc.gpsimd.wait_ge(gd.h, gd.v)
                    nc.all_engine_barrier()
            with ExitStack() as p9:
                xt9 = p9.enter_context(nc.sbuf_tensor(f"xt9{th}", [128, 2, E], F32))
                stb = p9.enter_context(nc.sbuf_tensor(f"stb{th}", [128, 2, E], F32))
                psT9 = [p9.enter_context(nc.psum_tensor(f"psT9{th}_{i}", [128, 1024], FP16))
                        for i in range(2)]
                bi9 = 0
                store_after = {}
                nc.tensor.wait_ge(t_ad.h, t_ad.v)
                for tt in range(4):
                    b = tt % 2
                    row0 = 512 * th + 128 * tt
                    if tt >= 2:
                        nc.sync.wait_ge(ev.h, ev.v)
                    sd.inc(nc.sync.dma_start(out=xt9[:, b, :],
                                             in_=x_in[row0:row0 + 128, :]), 16)
                    ldv = sd.v
                    for g in range(4):
                        psb = psT9[bi9 % 2]
                        if bi9 >= 2:
                            nc.tensor.wait_ge(ev.h, ev.v - 1)
                        for h8 in range(8):
                            hc = 8 * g + h8
                            ins = nc.tensor.transpose(
                                psb[:, 128 * h8:128 * h8 + 128],
                                bass.AP(OutAcc, hc * 512 + 128 * tt,
                                        [[NE * 512, 128], [1, 128]]),
                                identh[:, :])
                        pt.inc(ins)
                        nc.vector.wait_ge(pt.h, pt.v)
                        nc.vector.wait_ge(sd.h, ldv)
                        if g == 0 and tt >= 2:
                            nc.vector.wait_ge(t_st.h, store_after[tt - 2])
                        ev.inc(nc.vector.tensor_tensor(
                            stb[:, b, 1024 * g:1024 * g + 1024],
                            bass.AP(psb, 0, [[1024, 128], [1, 1024]]),
                            xt9[:, b, 1024 * g:1024 * g + 1024], ALU.add))
                        bi9 += 1
                    nc.sync.wait_ge(ev.h, ev.v)
                    t_st.inc(nc.sync.dma_start(out=out[row0:row0 + 128, :],
                                               in_=stb[:, b, :]), 16)
                    store_after[tt] = t_st.v
                nc.sync.wait_ge(t_st.h, t_st.v)
                nc.all_engine_barrier()

    nc.sync.wait_ge(t_st.h, t_st.v)
    big.close()
    return nc


def kernel(L0_attn_hidden_states, L0_attn_w_qkv, L0_attn_w_c, L0_ffn_ln_w,
           L0_ffn_gate_w, L0_ffn_up_w, L0_ffn_down_w):
    x = np.ascontiguousarray(np.asarray(L0_attn_hidden_states, dtype=np.float32)[0])
    wqkv = np.ascontiguousarray(np.asarray(L0_attn_w_qkv, dtype=np.float32))
    wc = np.ascontiguousarray(np.asarray(L0_attn_w_c, dtype=np.float32))
    lnw = np.ascontiguousarray(np.asarray(L0_ffn_ln_w, dtype=np.float32))
    w1 = np.ascontiguousarray(np.asarray(L0_ffn_gate_w, dtype=np.float32))
    w2 = np.ascontiguousarray(np.asarray(L0_ffn_up_w, dtype=np.float32))
    w3 = np.ascontiguousarray(np.asarray(L0_ffn_down_w, dtype=np.float32))

    nc = build_nc()
    in_maps = []
    for c in range(NCORE):
        in_maps.append({
            "x_own": np.ascontiguousarray(x[TOK * c:TOK * (c + 1)]),
            "w_qkv": wqkv, "w_c": wc, "ln_w": lnw,
            "gate_w": w1, "up_w": w2, "down_w": w3,
            "hids": np.array([[c, c + 8, c + 16, c + 24]], dtype=np.float32),
        })
    res = run_bass_kernel_spmd(nc, in_maps, core_ids=list(range(NCORE)))
    out = np.concatenate([np.asarray(res.results[c]["out"]) for c in range(NCORE)], axis=0)
    return out.reshape(1, S, E).astype(np.float32)


# version 07 (final)
# speedup vs baseline: 1.0631x; 1.0446x over previous
import math
from contextlib import ExitStack

import numpy as np
import ml_dtypes
import concourse.bass as bass
import concourse.mybir as mybir
from concourse.bass_utils import run_bass_kernel_spmd
from concourse.aot_env import allow_ici_all_to_all

F32 = mybir.dt.float32
BF16 = mybir.dt.bfloat16
FP16 = mybir.dt.float16
AF = mybir.ActivationFunctionType
ALU = mybir.AluOpType
AX = mybir.AxisListType

NCORE = 8
S = 8192
E = 4096
H = 32
D = 128
I = 14336
TOK = S // NCORE
NE = E // 128
NI = I // 128
NOC = (E + 2 * D) // 128
QSCALE = float(D) ** -0.5
EPS = 1e-5
LN2_4 = math.log(2.0) / 4.0
SKIP_THRESH = 150.0
SLAB = 28
NSLAB = NI // SLAB

DEBUG = {}


def chunks_for(qt, l):
    i0 = 128 * qt
    slope_min = 2.0 ** (-(8 * l + 8) / 4.0)
    cut = SKIP_THRESH / slope_min
    c_lo = max(0, int(math.ceil((i0 - 511 - cut) / 512.0)))
    return list(range(c_lo, i0 // 512 + 1))


class Ctr:
    __slots__ = ("h", "v")

    def __init__(self, nc, stack, name):
        self.h = stack.enter_context(nc.semaphore(name))
        self.v = 0

    def inc(self, inst, n=1):
        inst.then_inc(self.h, n)
        self.v += n
        return self.v

    def wait(self, eng, v=None):
        v = self.v if v is None else v
        if v > 0:
            eng.wait_ge(self.h, v)


def build_nc():
    nc = bass.Bass("TRN2", target_bir_lowering=False)

    x_in = nc.dram_tensor("x_own", [TOK, E], F32, kind="ExternalInput").ap()
    wqkv = nc.dram_tensor("w_qkv", [E, E + 2 * D], F32, kind="ExternalInput").ap()
    wc_in = nc.dram_tensor("w_c", [E, E], F32, kind="ExternalInput").ap()
    lnw_in = nc.dram_tensor("ln_w", [E], F32, kind="ExternalInput").ap()
    w1_in = nc.dram_tensor("gate_w", [I, E], F32, kind="ExternalInput").ap()
    w2_in = nc.dram_tensor("up_w", [I, E], F32, kind="ExternalInput").ap()
    w3_in = nc.dram_tensor("down_w", [E, I], F32, kind="ExternalInput").ap()
    hids = nc.dram_tensor("hids", [1, 4], F32, kind="ExternalInput").ap()
    out = nc.dram_tensor("out", [TOK, E], F32, kind="ExternalOutput").ap()

    kv_loc = nc.dram_tensor("kv_loc", [TOK, 2 * D], BF16).ap()
    kv_all = nc.dram_tensor("kv_all", [S, 2 * D], BF16).ap()
    q_loc = nc.dram_tensor("q_loc", [E, TOK], BF16).ap()
    q_all = nc.dram_tensor("q_all", [E, TOK], BF16).ap()
    o_loc = nc.dram_tensor("o_loc", [S, 4 * D], BF16).ap()
    attn_all = nc.dram_tensor("attn_all", [S, 4 * D], BF16).ap()
    s_dram = nc.dram_tensor("s_dram", [1, TOK], F32).ap()

    dbg = {}
    for name, shape, dt in DEBUG.get("outs", []):
        dbg[name] = nc.dram_tensor(name, shape, dt, kind="ExternalOutput").ap()

    ident_b = nc.inline_tensor(np.eye(128, dtype=ml_dtypes.bfloat16), "c_identb").ap()
    ident_h = nc.inline_tensor(np.eye(128, dtype=np.float16), "c_identh").ap()
    ident_f = nc.inline_tensor(np.eye(128, dtype=np.float32), "c_identf").ap()
    iota_c = nc.inline_tensor(np.arange(128, dtype=np.float32)[:, None], "c_iota").ap()
    mt_np = (np.arange(128)[:, None] <= np.arange(128)[None, :])
    maskT_c = nc.inline_tensor(mt_np.astype(ml_dtypes.bfloat16), "c_maskT").ap()
    ml_np = (np.arange(128)[:, None] >= np.arange(128)[None, :])
    maskL_c = nc.inline_tensor(ml_np.astype(ml_dtypes.bfloat16), "c_maskL").ap()
    onesr_c = nc.inline_tensor(np.ones((1, 128), dtype=np.float32), "c_onesr").ap()
    onescb_c = nc.inline_tensor(np.ones((128, 1), dtype=ml_dtypes.bfloat16), "c_onescb").ap()

    big = ExitStack()
    C = lambda name: Ctr(nc, big, name)
    sd = C("sd"); gd = C("gd"); st = C("st"); pt = C("pt"); ev = C("ev"); ccs = C("ccs")
    q_mm = C("q_mm"); q_ev = C("q_ev")
    a_sc = C("a_sc"); a_ex = C("a_ex"); a_tr = C("a_tr"); a_ev = C("a_ev")
    a_av = C("a_av"); a_bv = C("a_bv"); a_rc = C("a_rc"); a_rec = C("a_rec"); a_os = C("a_os"); a_ms = C("a_ms")
    t_tr = C("t_tr"); t_wg = C("t_wg"); t_mm = C("t_mm"); t_si = C("t_si")
    t_ym = C("t_ym"); t_dn = C("t_dn"); t_ad = C("t_ad"); t_st = C("t_st")

    sb = lambda name, shape, dt: big.enter_context(nc.sbuf_tensor(name, shape, dt))
    identb = sb("identb", [128, 128], BF16)
    identh = sb("identh", [128, 128], FP16)
    identf = sb("identf", [128, 128], F32)
    iotai = sb("iotai", [128, 1], F32)
    maskT = sb("maskT", [128, 128], BF16)
    maskL = sb("maskL", [128, 128], BF16)
    onesr = sb("onesr", [1, 128], F32)
    onescb = sb("onescb", [128, 1], BF16)
    lnw_sb = sb("lnw_sb", [128, NE], F32)
    slope_bc = sb("slope_bc", [128, 4], F32)
    nslope_bc = sb("nslope_bc", [128, 4], F32)
    wvec = sb("wvec", [128, 16], F32)
    s_bc = sb("s_bc", [128, TOK], F32)
    s_cols = sb("s_cols", [128, 8], F32)
    cvals = sb("cvals", [128, 2], F32)
    xn_fm = sb("xn_fm", [128, NE, TOK], BF16)

    def xn_ap(k, c0, n):
        return bass.AP(xn_fm, k * TOK + c0, [[NE * TOK, 128], [1, n]])

    def ecopy(eng, out_ap, in_ap):
        return nc.scalar.copy(out_ap, in_ap) if eng is nc.scalar else eng.tensor_copy(out_ap, in_ap)

    sd.inc(nc.sync.dma_start(out=identb[:, :], in_=ident_b), 16)
    sd.inc(nc.sync.dma_start(out=identh[:, :], in_=ident_h), 16)
    sd.inc(nc.sync.dma_start(out=identf[:, :], in_=ident_f), 16)
    sd.inc(nc.sync.dma_start(out=iotai[:, :], in_=iota_c), 16)
    sd.inc(nc.sync.dma_start(out=maskT[:, :], in_=maskT_c), 16)
    sd.inc(nc.sync.dma_start(out=maskL[:, :], in_=maskL_c), 16)
    sd.inc(nc.sync.dma_start(out=onesr[:, :], in_=onesr_c), 16)
    sd.inc(nc.sync.dma_start(out=onescb[:, :], in_=onescb_c), 16)
    gd.inc(nc.gpsimd.dma_start(out=lnw_sb[:, :],
                               in_=bass.AP(lnw_in.tensor, 0, [[1, 128], [128, NE]]),
                               allow_slow_non_contiguous=True), 16)

    with nc.psum_tensor("ps0", [128, 512], F32) as ps0, \
         nc.sbuf_tensor("hrow", [1, 4], F32) as hrow, \
         nc.sbuf_tensor("srow4", [1, 4], F32) as srow4, \
         nc.sbuf_tensor("wtmp", [128, 1], F32) as wtmp:
        sd.inc(nc.sync.dma_start(out=hrow[:, :], in_=hids), 16)
        st.inc(nc.vector.memset(cvals[:, 0:1], -LN2_4))
        st.wait(nc.vector)
        st.inc(nc.vector.memset(cvals[:, 1:2], EPS))
        sd.wait(nc.scalar); gd.wait(nc.scalar); st.wait(nc.scalar)
        st.inc(nc.scalar.activation(srow4[:, :], hrow[:, :], AF.Exp,
                                    bias=cvals[0:1, 0:1], scale=-LN2_4))
        st.wait(nc.tensor); sd.wait(nc.tensor)
        st.inc(nc.tensor.matmul(ps0[:, 0:4], onesr[0:1, :], srow4[:, :],
                                start=True, stop=True))
        st.wait(nc.vector)
        st.inc(nc.vector.tensor_copy(slope_bc[:, :], ps0[:, 0:4]))
        st.wait(nc.vector)
        st.inc(nc.vector.tensor_scalar_mul(nslope_bc[:, :], slope_bc[:, :], -1.0))
        for l in range(4):
            for s_ in range(4):
                off = -64.0 if l == 0 else float(128 * s_ - 256)
                st.wait(nc.vector)
                st.inc(nc.vector.tensor_scalar(wtmp[:, :], iotai[:, :], off,
                                               slope_bc[:, l:l + 1], ALU.add, ALU.mult))
                st.wait(nc.scalar)
                st.inc(nc.scalar.activation(wvec[:, 4 * l + s_:4 * l + s_ + 1],
                                            wtmp[:, :], AF.Exp))
        nc.all_engine_barrier()

        with nc.sbuf_tensor("xt", [128, 2, E], F32) as xt, \
             nc.sbuf_tensor("sqscr", [128, E], BF16) as sqscr, \
             nc.sbuf_tensor("stat", [128, 10], F32) as stat, \
             nc.sbuf_tensor("xnt", [128, E], BF16) as xnt, \
             nc.sbuf_tensor("s8", [8, 128], F32) as s8, \
             nc.sbuf_tensor("srowT", [1, TOK], F32) as srowT, \
             nc.psum_tensor("psT1a", [128, 1024], BF16) as psT1a, \
             nc.psum_tensor("psT1b", [128, 1024], BF16) as psT1b:
            c_sum = stat[:, 0:1]; c_ssq = stat[:, 1:2]; c_m = stat[:, 2:3]
            c_e2 = stat[:, 3:4]; c_m2 = stat[:, 4:5]; c_var = stat[:, 5:6]
            c_ln = stat[:, 6:7]; c_r = stat[:, 7:8]; c_rr = stat[:, 8:9]
            c_mx = stat[:, 9:10]
            xn_done_st = [0] * 8
            bi = 0
            for tt in range(8):
                b = tt % 2
                if tt >= 2:
                    nc.sync.wait_ge(st.h, xn_done_st[tt - 2])
                sd.inc(nc.sync.dma_start(out=xt[:, b, :],
                                         in_=x_in[128 * tt:128 * tt + 128, :]), 16)
                nc.vector.wait_ge(sd.h, sd.v)
                nc.scalar.wait_ge(sd.h, sd.v)
                st.inc(nc.vector.reduce_sum(c_sum, xt[:, b, :], AX.X))
                st.wait(nc.scalar)
                st.inc(nc.scalar.activation(sqscr[:, :], xt[:, b, :], AF.Square,
                                            accum_out=c_ssq))
                st.wait(nc.vector)
                st.inc(nc.vector.tensor_scalar_mul(c_m, c_sum, 1.0 / E))
                st.wait(nc.vector)
                st.inc(nc.vector.tensor_scalar_mul(c_e2, c_ssq, 1.0 / E))
                st.wait(nc.vector)
                st.inc(nc.vector.tensor_mul(c_m2, c_m, c_m))
                st.wait(nc.vector)
                st.inc(nc.vector.tensor_sub(c_var, c_e2, c_m2))
                st.wait(nc.scalar)
                st.inc(nc.scalar.activation(c_ln, c_var, AF.Ln, bias=cvals[:, 1:2]))
                st.wait(nc.scalar)
                st.inc(nc.scalar.activation(c_r, c_ln, AF.Exp, scale=-0.5))
                st.wait(nc.scalar)
                st.inc(nc.scalar.activation(c_rr, c_ln, AF.Exp, scale=-1.0))
                st.wait(nc.vector)
                st.inc(nc.vector.tensor_mul(c_mx, c_var, c_rr))
                st.wait(nc.scalar)
                st.inc(nc.scalar.activation(c_ln, c_mx, AF.Ln, bias=cvals[:, 1:2]))
                st.wait(nc.scalar)
                st.inc(nc.scalar.activation(s_cols[:, tt:tt + 1], c_ln, AF.Exp, scale=-0.5))
                st.wait(nc.vector)
                if tt >= 1:
                    nc.vector.wait_ge(pt.h, 4 * tt)
                st.inc(nc.vector.tensor_scalar(xnt[:, :], xt[:, b, :], c_m, c_r,
                                               ALU.subtract, ALU.mult))
                xn_done_st[tt] = st.v
                st.wait(nc.tensor)
                for g in range(4):
                    psb = psT1a if bi % 2 == 0 else psT1b
                    if bi >= 2:
                        nc.tensor.wait_ge(ev.h, bi - 1)
                    for k8 in range(8):
                        k = 8 * g + k8
                        ins = nc.tensor.transpose(psb[:, 128 * k8:128 * k8 + 128],
                                                  xnt[:, 128 * k:128 * k + 128],
                                                  identb[:, :])
                    pt.inc(ins)
                    eng = nc.vector if bi % 2 == 0 else nc.scalar
                    eng.wait_ge(pt.h, pt.v)
                    dst = bass.AP(xn_fm, TOK * 8 * g + 128 * tt,
                                  [[NE * TOK, 128], [TOK, 8], [1, 128]])
                    ev.inc(ecopy(eng,
                        dst, bass.AP(psb, 0, [[1024, 128], [128, 8], [1, 128]])))
                    bi += 1
            nc.tensor.wait_ge(st.h, st.v)
            st.inc(nc.tensor.transpose(ps0[0:8, 0:128], s_cols[:, :], identf[:, :]))
            st.wait(nc.vector)
            st.inc(nc.vector.tensor_copy(s8[:, :], ps0[0:8, 0:128]))
            st.wait(nc.sync)
            sd.inc(nc.sync.dma_start(out=bass.AP(s_dram.tensor, 0, [[128, 8], [1, 128]]),
                                     in_=s8[:, :]), 16)
            nc.sync.wait_ge(sd.h, sd.v)
            sd.inc(nc.sync.dma_start(out=srowT[:, :], in_=s_dram), 16)
            nc.tensor.wait_ge(sd.h, sd.v)
            for t2 in range(2):
                if t2 == 1:
                    nc.tensor.wait_ge(st.h, st.v)
                st.inc(nc.tensor.matmul(ps0[:, :], onesr[0:1, :],
                                        srowT[:, 512 * t2:512 * t2 + 512],
                                        start=True, stop=True))
                st.wait(nc.vector)
                st.inc(nc.vector.tensor_copy(s_bc[:, 512 * t2:512 * t2 + 512], ps0[:, :]))
            nc.all_engine_barrier()
            if "d_xn" in dbg:
                gd.inc(nc.gpsimd.dma_start(
                    out=dbg["d_xn"],
                    in_=bass.AP(xn_fm, 0, [[NE * TOK, 128], [1, NE * TOK]])), 16)
                nc.gpsimd.wait_ge(gd.h, gd.v)
                nc.all_engine_barrier()

    with ExitStack() as p2:
        wcb = p2.enter_context(nc.sbuf_tensor("wcb", [128, 2, NE, 128], BF16))
        qstg = p2.enter_context(nc.sbuf_tensor("qstg", [128, 2, 512], BF16))
        kvfm = p2.enter_context(nc.sbuf_tensor("kvfm", [128, 2, TOK], BF16))
        kvtok = p2.enter_context(nc.sbuf_tensor("kvtok", [128, 8, 256], BF16))
        psQ = [p2.enter_context(nc.psum_tensor(f"psQ{i}", [128, 512], F32)) for i in range(2)]
        psP2 = [p2.enter_context(nc.psum_tensor(f"psP2{i}", [128, 1024], BF16)) for i in range(2)]

        q_store_sd = {}
        nc.tensor.wait_ge(ev.h, ev.v)
        nc.tensor.wait_ge(st.h, st.v)
        for oc in range(NOC):
            b = oc % 2
            if oc >= 2:
                nc.gpsimd.wait_ge(q_mm.h, 2 * (oc - 2) + 2)
            for qq in range(4):
                src = bass.AP(wqkv.tensor, 128 * oc + (E + 2 * D) * 128 * 8 * qq,
                              [[E + 2 * D, 128], [(E + 2 * D) * 128, 8], [1, 128]])
                gd.inc(nc.gpsimd.dma_start(out=wcb[:, b, 8 * qq:8 * qq + 8, :], in_=src), 16)
            for t in range(2):
                u2 = 2 * oc + t
                nc.tensor.wait_ge(gd.h, gd.v)
                if u2 >= 2:
                    nc.tensor.wait_ge(q_ev.h, u2 - 1)
                for k in range(NE):
                    ins = nc.tensor.matmul(psQ[t][:, :], wcb[:, b, k, :],
                                           xn_ap(k, 512 * t, 512),
                                           start=(k == 0), stop=(k == NE - 1))
                q_mm.inc(ins)
                nc.vector.wait_ge(q_mm.h, u2 + 1)
                if oc < 32:
                    if u2 >= 2 and (u2 - 2) in q_store_sd:
                        nc.vector.wait_ge(sd.h, q_store_sd[u2 - 2])
                    q_ev.inc(nc.vector.tensor_scalar_mul(qstg[:, u2 % 2, :],
                                                         psQ[t][:, :], QSCALE))
                    nc.sync.wait_ge(q_ev.h, u2 + 1)
                    row0 = 512 * (oc % 8) + 128 * (oc // 8)
                    sd.inc(nc.sync.dma_start(
                        out=bass.AP(q_loc.tensor, row0 * TOK + 512 * t,
                                    [[TOK, 128], [1, 512]]),
                        in_=qstg[:, u2 % 2, :]), 16)
                    q_store_sd[u2] = sd.v
                else:
                    q_ev.inc(nc.vector.tensor_copy(
                        bass.AP(kvfm, (oc - 32) * TOK + 512 * t, [[2 * TOK, 128], [1, 512]]),
                        psQ[t][:, :]))
        nc.all_engine_barrier()
        nc.tensor.wait_ge(q_ev.h, q_ev.v)
        for tt in range(8):
            bb = tt % 2
            psb = psP2[bb]
            if tt >= 2:
                nc.tensor.wait_ge(ev.h, ev.v - 1)
            nc.tensor.transpose(psb[:, 0:128],
                                bass.AP(kvfm, 128 * tt, [[2 * TOK, 128], [1, 128]]),
                                identb[:, :])
            ins = nc.tensor.transpose(psb[:, 128:256],
                                      bass.AP(kvfm, TOK + 128 * tt, [[2 * TOK, 128], [1, 128]]),
                                      identb[:, :])
            pt.inc(ins)
            nc.vector.wait_ge(pt.h, pt.v)
            ev.inc(nc.vector.tensor_copy(kvtok[:, tt, :], psb[:, 0:256]))
        nc.sync.wait_ge(ev.h, ev.v)
        sd.inc(nc.sync.dma_start(
            out=bass.AP(kv_loc.tensor, 0, [[256, 128], [256 * 128, 8], [1, 256]]),
            in_=kvtok[:, :, :]), 16)
        nc.sync.wait_ge(sd.h, sd.v)
        nc.all_engine_barrier()

    ccs.inc(nc.gpsimd.collective_compute(
        "AllGather", ALU.bypass, replica_groups=[list(range(NCORE))],
        ins=[kv_loc.opt()], outs=[kv_all.opt()]))
    with allow_ici_all_to_all():
        ccs.inc(nc.gpsimd.collective_compute(
            "AllToAll", ALU.bypass, replica_groups=[list(range(NCORE))],
            ins=[q_loc.opt()], outs=[q_all.opt()]))
    nc.gpsimd.wait_ge(ccs.h, ccs.v)
    nc.all_engine_barrier()
    if "d_kv" in dbg:
        gd.inc(nc.gpsimd.dma_start(out=dbg["d_kv"], in_=kv_all), 16)
        nc.gpsimd.wait_ge(gd.h, gd.v)
    if "d_q" in dbg:
        gd.inc(nc.gpsimd.dma_start(out=dbg["d_q"], in_=q_all), 16)
        nc.gpsimd.wait_ge(gd.h, gd.v)

    with ExitStack() as p4:
        qg = p4.enter_context(nc.sbuf_tensor("qg", [128, 4, 8, TOK], BF16))
        K_sb = p4.enter_context(nc.sbuf_tensor("K_sb", [128, S], BF16))
        V_sb = p4.enter_context(nc.sbuf_tensor("V_sb", [128, 64, 128], BF16))
        prob = p4.enter_context(nc.sbuf_tensor("prob", [128, 2, 512], BF16))
        ptb = p4.enter_context(nc.sbuf_tensor("ptb", [128, 2, 512], BF16))
        bvec = p4.enter_context(nc.sbuf_tensor("bvec", [128, 2, 4], F32))
        recb = p4.enter_context(nc.sbuf_tensor("recb", [128, 2], F32))
        ostg = p4.enter_context(nc.sbuf_tensor("ostg", [128, 2, 512], BF16))
        psS = [p4.enter_context(nc.psum_tensor(f"psS{i}", [128, 512], F32)) for i in range(2)]
        psPT = [p4.enter_context(nc.psum_tensor(f"psPT{i}", [128, 1024], BF16)) for i in range(2)]
        psO = [p4.enter_context(nc.psum_tensor(f"psO{i}", [128, 512], F32)) for i in range(2)]
        psD = [p4.enter_context(nc.psum_tensor(f"psD{i}", [128, 512], F32)) for i in range(2)]

        sd.inc(nc.sync.dma_start(out=K_sb[:, :], in_=kv_all[:, 0:128], transpose=True), 16)
        sd.inc(nc.sync.dma_start(
            out=V_sb[:, :, :],
            in_=bass.AP(kv_all.tensor, 128, [[256, 128], [256 * 128, 64], [1, 128]])), 16)
        for hl4 in range(4):
            sd.inc(nc.sync.dma_start(
                out=qg[:, hl4, :, :],
                in_=bass.AP(q_all.tensor, 128 * TOK * hl4,
                            [[TOK, 128], [512 * TOK, 8], [1, TOK]])), 16)
        nc.sync.wait_ge(sd.h, sd.v)
        nc.all_engine_barrier()

        units = []
        for qt in range(64):
            for hli in range(4):
                cl = chunks_for(qt, hli)
                for ci, c in enumerate(cl):
                    diag = (c == qt // 4)
                    ns = (qt % 4) + 1 if diag else 4
                    units.append((qt, hli, c, ns, diag, ci == 0, ci == len(cl) - 1))
        U = len(units)
        a_os_after = {}

        for u in range(U + 2):
            if u < U:
                qt, l, c, ns, diag, first, last = units[u]
                i0 = 128 * qt
                nc.tensor.wait_ge(a_ex.h, max(0, u - 1))
                ins = nc.tensor.matmul(psS[u % 2][:, 0:128 * ns],
                                       qg[:, l, qt // 8, 128 * (qt % 8):128 * (qt % 8) + 128],
                                       K_sb[:, 512 * c:512 * c + 128 * ns],
                                       start=True, stop=True)
                a_sc.inc(ins)
                nc.vector.wait_ge(a_ex.h, max(0, u - 1))
                if l == 0:
                    for s_ in range(ns):
                        jc = 512 * c + 128 * s_ + 64
                        ins = nc.vector.tensor_scalar(bvec[:, u % 2, s_:s_ + 1], iotai[:, :],
                                                      float(i0 - jc), nslope_bc[:, 0:1],
                                                      ALU.add, ALU.mult)
                else:
                    jc = 512 * c + 256
                    ins = nc.vector.tensor_scalar(bvec[:, u % 2, 0:1], iotai[:, :],
                                                  float(i0 - jc), nslope_bc[:, l:l + 1],
                                                  ALU.add, ALU.mult)
                a_bv.inc(ins)
                nc.scalar.wait_ge(a_sc.h, u + 1)
                nc.scalar.wait_ge(a_bv.h, u + 1)
                nc.scalar.wait_ge(a_tr.h, max(0, u - 1))
                if l == 0:
                    for s_ in range(ns):
                        ins = nc.scalar.activation(prob[:, u % 2, 128 * s_:128 * s_ + 128],
                                                   psS[u % 2][:, 128 * s_:128 * s_ + 128],
                                                   AF.Exp, bias=bvec[:, u % 2, s_:s_ + 1])
                else:
                    ins = nc.scalar.activation(prob[:, u % 2, 0:128 * ns],
                                               psS[u % 2][:, 0:128 * ns],
                                               AF.Exp, bias=bvec[:, u % 2, 0:1])
                a_ex.inc(ins)
            if 1 <= u <= U:
                v = u - 1
                qt, l, c, ns, diag, first, last = units[v]
                e = 4 * qt + l
                if diag:
                    nc.vector.wait_ge(a_ex.h, v + 1)
                    sdiag = ns - 1
                    a_ms.inc(nc.vector.tensor_mul(
                        prob[:, v % 2, 128 * sdiag:128 * sdiag + 128],
                        prob[:, v % 2, 128 * sdiag:128 * sdiag + 128],
                        maskL[:, :]))
                nc.tensor.wait_ge(a_ex.h, v + 1)
                if diag:
                    nc.tensor.wait_ge(a_ms.h, a_ms.v)
                nc.tensor.wait_ge(a_ev.h, max(0, v - 1))
                for s_ in range(ns):
                    ins = nc.tensor.transpose(psPT[v % 2][:, 128 * s_:128 * s_ + 128],
                                              prob[:, v % 2, 128 * s_:128 * s_ + 128],
                                              identb[:, :])
                a_tr.inc(ins)
                nc.vector.wait_ge(a_tr.h, v + 1)
                nc.vector.wait_ge(a_av.h, max(0, v - 1))
                for s_ in range(ns):
                    wcol = wvec[:, 4 * l:4 * l + 1] if l == 0 else wvec[:, 4 * l + s_:4 * l + s_ + 1]
                    dstp = ptb[:, v % 2, 128 * s_:128 * s_ + 128]
                    srcp = psPT[v % 2][:, 128 * s_:128 * s_ + 128]
                    ins = nc.vector.tensor_scalar(dstp, srcp, wcol, None, ALU.mult)
                a_ev.inc(ins)
            if u >= 2:
                v2 = u - 2
                qt, l, c, ns, diag, first, last = units[v2]
                e = 4 * qt + l
                if first:
                    nc.tensor.wait_ge(a_rec.h, max(0, e - 1))
                nc.tensor.wait_ge(a_ev.h, v2 + 1)
                for s_ in range(ns):
                    jt = 4 * c + s_
                    nc.tensor.matmul(psO[e % 2][:, 0:128],
                                     ptb[:, v2 % 2, 128 * s_:128 * s_ + 128],
                                     V_sb[:, jt, :],
                                     start=(first and s_ == 0), stop=(last and s_ == ns - 1))
                for s_ in range(ns):
                    ins = nc.tensor.matmul(psD[e % 2][:, 0:1],
                                           ptb[:, v2 % 2, 128 * s_:128 * s_ + 128],
                                           onescb[:, :],
                                           start=(first and s_ == 0), stop=(last and s_ == ns - 1))
                a_av.inc(ins)
                if last:
                    nc.vector.wait_ge(a_av.h, v2 + 1)
                    a_rc.inc(nc.vector.reciprocal(recb[:, e % 2:e % 2 + 1],
                                                  psD[e % 2][:, 0:1]))
                    nc.vector.wait_ge(a_rc.h, e + 1)
                    if l == 0 and qt >= 2:
                        nc.vector.wait_ge(a_os.h, a_os_after[qt - 2])
                    a_rec.inc(nc.vector.tensor_scalar(
                        ostg[:, qt % 2, 128 * l:128 * l + 128], psO[e % 2][:, 0:128],
                        recb[:, e % 2:e % 2 + 1], None, ALU.mult))
                    if l == 3:
                        nc.sync.wait_ge(a_rec.h, e + 1)
                        a_os.inc(nc.sync.dma_start(
                            out=bass.AP(o_loc.tensor, 128 * qt * 512, [[512, 128], [1, 512]]),
                            in_=ostg[:, qt % 2, :]), 16)
                        a_os_after[qt] = a_os.v
        nc.sync.wait_ge(a_os.h, a_os.v)
        nc.all_engine_barrier()

    with allow_ici_all_to_all():
        ccs.inc(nc.gpsimd.collective_compute(
            "AllToAll", ALU.bypass, replica_groups=[list(range(NCORE))],
            ins=[o_loc.opt()], outs=[attn_all.opt()]))
    nc.gpsimd.wait_ge(ccs.h, ccs.v)
    nc.all_engine_barrier()
    if "d_attn" in dbg:
        gd.inc(nc.gpsimd.dma_start(out=dbg["d_attn"], in_=attn_all), 16)
        nc.gpsimd.wait_ge(gd.h, gd.v)

    for k in range(NE):
        st.inc(nc.vector.scalar_tensor_tensor(
            bass.AP(xn_fm, k * TOK, [[NE * TOK, 128], [1, TOK]]),
            s_bc[:, :], lnw_sb[:, k:k + 1],
            bass.AP(xn_fm, k * TOK, [[NE * TOK, 128], [1, TOK]]),
            ALU.mult, ALU.mult))
    nc.all_engine_barrier()

    for th in range(2):
        with ExitStack() as pt6:
            OutAcc = pt6.enter_context(nc.sbuf_tensor(f"OutAcc{th}", [128, NE, 512], FP16))
            with ExitStack() as p6a:
                att_ld = p6a.enter_context(nc.sbuf_tensor(f"att_ld{th}", [128, 2, 8, 4, 128], BF16))
                attn_sb = p6a.enter_context(nc.sbuf_tensor(f"attn_sb{th}", [128, NE, 512], BF16))
                psT6 = [p6a.enter_context(nc.psum_tensor(f"psT6{th}_{i}", [128, 1024], BF16))
                        for i in range(2)]
                bi6 = 0
                ld_sd = {}
                for tt in range(4):
                    b = tt % 2
                    if tt >= 2:
                        nc.sync.wait_ge(pt.h, pt.v)
                    for hl4 in range(4):
                        sd.inc(nc.sync.dma_start(
                            out=att_ld[:, b, :, hl4, :],
                            in_=bass.AP(attn_all.tensor, (512 * th + 128 * tt) * 512 + 128 * hl4,
                                        [[512, 128], [1024 * 512, 8], [1, 128]])), 16)
                    ld_sd[tt] = sd.v
                    nc.tensor.wait_ge(sd.h, ld_sd[tt])
                    for g in range(4):
                        psb = psT6[bi6 % 2]
                        if bi6 >= 2:
                            nc.tensor.wait_ge(ev.h, ev.v - 1)
                        for h8 in range(8):
                            h = 8 * g + h8
                            ins = nc.tensor.transpose(
                                psb[:, 128 * h8:128 * h8 + 128],
                                att_ld[:, b, h % 8, h // 8, :], identb[:, :])
                        pt.inc(ins)
                        eng = nc.vector if bi6 % 2 == 0 else nc.scalar
                        eng.wait_ge(pt.h, pt.v)
                        dst = bass.AP(attn_sb, 512 * 8 * g + 128 * tt,
                                      [[NE * 512, 128], [512, 8], [1, 128]])
                        ev.inc(ecopy(eng,
                            dst, bass.AP(psb, 0, [[1024, 128], [128, 8], [1, 128]])))
                        bi6 += 1
                nc.all_engine_barrier()
                wcb2 = p6a.enter_context(nc.sbuf_tensor(f"wcb2{th}", [128, 2, NE, 128], BF16))
                psF6 = [p6a.enter_context(nc.psum_tensor(f"psF6{th}_{i}", [128, 512], F32))
                        for i in range(2)]
                nc.tensor.wait_ge(ev.h, ev.v)
                wc_ld = {}
                for pc in range(2):
                    for qq in range(4):
                        srcp2 = bass.AP(wc_in.tensor, 128 * pc + E * 128 * 8 * qq,
                                        [[E, 128], [E * 128, 8], [1, 128]])
                        gd.inc(nc.gpsimd.dma_start(out=wcb2[:, pc, 8 * qq:8 * qq + 8, :],
                                                   in_=srcp2), 16)
                    wc_ld[pc] = gd.v
                for oc in range(NE):
                    b = oc % 2
                    nc.tensor.wait_ge(gd.h, wc_ld[oc])
                    if oc >= 2:
                        nc.tensor.wait_ge(t_ad.h, t_ad.v - 1)
                    for hh in range(NE):
                        ins = nc.tensor.matmul(psF6[b][:, :], wcb2[:, b, hh, :],
                                               bass.AP(attn_sb, hh * 512,
                                                       [[NE * 512, 128], [1, 512]]),
                                               start=(hh == 0), stop=(hh == NE - 1))
                    t_mm.inc(ins)
                    if oc + 2 < NE:
                        nc.gpsimd.wait_ge(t_mm.h, t_mm.v)
                        for qq in range(4):
                            srcp2 = bass.AP(wc_in.tensor, 128 * (oc + 2) + E * 128 * 8 * qq,
                                            [[E, 128], [E * 128, 8], [1, 128]])
                            gd.inc(nc.gpsimd.dma_start(out=wcb2[:, b, 8 * qq:8 * qq + 8, :],
                                                       in_=srcp2), 16)
                        wc_ld[oc + 2] = gd.v
                    nc.vector.wait_ge(t_mm.h, t_mm.v)
                    t_ad.inc(nc.vector.tensor_copy(
                        bass.AP(OutAcc, oc * 512, [[NE * 512, 128], [1, 512]]),
                        psF6[b][:, :]))
                nc.all_engine_barrier()
                if "d_acc1" in dbg:
                    gd.inc(nc.gpsimd.dma_start(
                        out=bass.AP(dbg["d_acc1"].tensor, th * NE * 512,
                                    [[2 * NE * 512, 128], [1, NE * 512]]),
                        in_=bass.AP(OutAcc, 0, [[NE * 512, 128], [1, NE * 512]])), 16)
                    nc.gpsimd.wait_ge(gd.h, gd.v)
                    nc.all_engine_barrier()
            with ExitStack() as p8:
                w1b = p8.enter_context(nc.sbuf_tensor(f"w1b{th}", [128, 2, NE, 128], BF16))
                w2b = p8.enter_context(nc.sbuf_tensor(f"w2b{th}", [128, 2, NE, 128], BF16))
                w1t = p8.enter_context(nc.sbuf_tensor(f"w1t{th}", [128, NE, 128], BF16))
                w2t = p8.enter_context(nc.sbuf_tensor(f"w2t{th}", [128, NE, 128], BF16))
                w3b = p8.enter_context(nc.sbuf_tensor(f"w3b{th}", [128, 2, SLAB, 128], BF16))
                w3t = p8.enter_context(nc.sbuf_tensor(f"w3t{th}", [128, SLAB, 128], BF16))
                gs_sb = p8.enter_context(nc.sbuf_tensor(f"gs_sb{th}", [128, 2, 512], BF16))
                Y_sb = p8.enter_context(nc.sbuf_tensor(f"Y_sb{th}", [128, SLAB, 512], BF16))
                psT8 = [p8.enter_context(nc.psum_tensor(f"psT8{th}_{i}", [128, 1024], BF16))
                        for i in range(2)]
                psG = p8.enter_context(nc.psum_tensor(f"psG{th}", [128, 512], F32))
                psU = p8.enter_context(nc.psum_tensor(f"psU{th}", [128, 512], F32))
                psF8 = [p8.enter_context(nc.psum_tensor(f"psF8{th}_{i}", [128, 512], F32))
                        for i in range(2)]
                bi8 = 0
                tw_after_ic = {}

                def transpose_block(src3, bidx, n, wdst, ld_val):
                    nonlocal bi8
                    nb = (n + 7) // 8
                    for g in range(nb):
                        kk = list(range(8 * g, min(8 * g + 8, n)))
                        psb = psT8[bi8 % 2]
                        if bi8 >= 2:
                            nc.tensor.wait_ge(t_wg.h, t_wg.v - 1)
                        nc.tensor.wait_ge(gd.h, ld_val)
                        for j, k in enumerate(kk):
                            if bidx is None:
                                ins = nc.tensor.transpose(psb[:, 128 * j:128 * j + 128],
                                                          src3[:, k, :], identb[:, :])
                            else:
                                ins = nc.tensor.transpose(psb[:, 128 * j:128 * j + 128],
                                                          src3[:, bidx, k, :], identb[:, :])
                        t_tr.inc(ins)
                        eng = nc.vector if bi8 % 2 == 0 else nc.scalar
                        eng.wait_ge(t_tr.h, t_tr.v)
                        dst = bass.AP(wdst, 128 * 8 * g,
                                      [[wdst.shape[1] * 128, 128], [128, len(kk)], [1, 128]])
                        t_wg.inc(ecopy(eng,
                            dst, bass.AP(psb, 0, [[1024, 128], [128, len(kk)], [1, 128]])))
                        bi8 += 1

                nc.tensor.wait_ge(st.h, st.v)
                nc.tensor.wait_ge(t_ad.h, t_ad.v)
                w12_ld = {}

                def issue_w12(gi2):
                    b2 = gi2 % 2
                    src1 = bass.AP(w1_in.tensor, 128 * gi2 * E,
                                   [[E, 128], [128, NE], [1, 128]])
                    gd.inc(nc.gpsimd.dma_start(out=w1b[:, b2, :, :], in_=src1), 16)
                    l1 = gd.v
                    src2 = bass.AP(w2_in.tensor, 128 * gi2 * E,
                                   [[E, 128], [128, NE], [1, 128]])
                    gd.inc(nc.gpsimd.dma_start(out=w2b[:, b2, :, :], in_=src2), 16)
                    w12_ld[gi2] = (l1, gd.v)

                issue_w12(0)
                for slab in range(NSLAB):
                    for icl in range(SLAB):
                        gi = SLAB * slab + icl
                        b = gi % 2
                        if gi + 1 < NI:
                            if gi >= 1:
                                nc.gpsimd.wait_ge(t_wg.h, tw_after_ic[gi - 1])
                            issue_w12(gi + 1)
                        ld1, ld2 = w12_ld[gi]
                        transpose_block(w1b, b, NE, w1t, ld1)
                        g_evc = t_wg.v
                        transpose_block(w2b, b, NE, w2t, ld2)
                        nc.tensor.wait_ge(t_wg.h, g_evc)
                        if gi >= 1:
                            nc.tensor.wait_ge(t_si.h, t_si.v)
                        for k in range(NE):
                            ins = nc.tensor.matmul(psG[:, :], w1t[:, k, :],
                                                   xn_ap(k, 512 * th, 512),
                                                   start=(k == 0), stop=(k == NE - 1))
                        t_mm.inc(ins)
                        nc.scalar.wait_ge(t_mm.h, t_mm.v)
                        if gi >= 2:
                            nc.scalar.wait_ge(t_ym.h, t_ym.v - 1)
                        t_si.inc(nc.scalar.activation(gs_sb[:, b, :], psG[:, :], AF.Silu))
                        nc.tensor.wait_ge(t_wg.h, t_wg.v)
                        if gi >= 1:
                            nc.tensor.wait_ge(t_ym.h, t_ym.v)
                        for k in range(NE):
                            ins = nc.tensor.matmul(psU[:, :], w2t[:, k, :],
                                                   xn_ap(k, 512 * th, 512),
                                                   start=(k == 0), stop=(k == NE - 1))
                        t_mm.inc(ins)
                        nc.vector.wait_ge(t_mm.h, t_mm.v)
                        nc.vector.wait_ge(t_si.h, t_si.v)
                        if slab >= 1 and icl == 0:
                            nc.vector.wait_ge(t_dn.h, t_dn.v)
                        t_ym.inc(nc.vector.tensor_tensor(Y_sb[:, icl, :], psU[:, :],
                                                         gs_sb[:, b, :], ALU.mult))
                        tw_after_ic[gi] = t_wg.v
                    w3_ld = {}

                    def issue_w3(hc2):
                        src3s = bass.AP(w3_in.tensor, 128 * hc2 * I + 128 * SLAB * slab,
                                        [[I, 128], [128, SLAB], [1, 128]])
                        gd.inc(nc.gpsimd.dma_start(out=w3b[:, hc2 % 2, :, :], in_=src3s), 16)
                        w3_ld[hc2] = gd.v

                    nc.gpsimd.wait_ge(t_wg.h, t_wg.v)
                    issue_w3(0)
                    tw_w3 = {}
                    for hc in range(NE):
                        if hc + 1 < NE:
                            if hc >= 1:
                                nc.gpsimd.wait_ge(t_wg.h, tw_w3[hc - 1])
                            issue_w3(hc + 1)
                        transpose_block(w3b[:, hc % 2], None, SLAB, w3t, w3_ld[hc])
                        tw_w3[hc] = t_wg.v
                        bb = hc % 2
                        nc.tensor.wait_ge(t_wg.h, t_wg.v)
                        nc.tensor.wait_ge(t_ym.h, t_ym.v)
                        nc.tensor.wait_ge(t_ad.h, max(0, t_ad.v - 1))
                        for icl in range(SLAB):
                            ins = nc.tensor.matmul(psF8[bb][:, :], w3t[:, icl, :],
                                                   Y_sb[:, icl, :],
                                                   start=(icl == 0), stop=(icl == SLAB - 1))
                        t_dn.inc(ins)
                        nc.vector.wait_ge(t_dn.h, t_dn.v)
                        t_ad.inc(nc.vector.tensor_tensor(
                            bass.AP(OutAcc, hc * 512, [[NE * 512, 128], [1, 512]]),
                            psF8[bb][:, :],
                            bass.AP(OutAcc, hc * 512, [[NE * 512, 128], [1, 512]]),
                            ALU.add))
                nc.all_engine_barrier()
                if "d_acc2" in dbg:
                    gd.inc(nc.gpsimd.dma_start(
                        out=bass.AP(dbg["d_acc2"].tensor, th * NE * 512,
                                    [[2 * NE * 512, 128], [1, NE * 512]]),
                        in_=bass.AP(OutAcc, 0, [[NE * 512, 128], [1, NE * 512]])), 16)
                    nc.gpsimd.wait_ge(gd.h, gd.v)
                    nc.all_engine_barrier()
            with ExitStack() as p9:
                xt9 = p9.enter_context(nc.sbuf_tensor(f"xt9{th}", [128, 2, E], F32))
                stb = p9.enter_context(nc.sbuf_tensor(f"stb{th}", [128, 2, E], F32))
                psT9 = [p9.enter_context(nc.psum_tensor(f"psT9{th}_{i}", [128, 1024], FP16))
                        for i in range(2)]
                bi9 = 0
                store_after = {}
                nc.tensor.wait_ge(t_ad.h, t_ad.v)
                for tt in range(4):
                    b = tt % 2
                    row0 = 512 * th + 128 * tt
                    if tt >= 2:
                        nc.sync.wait_ge(ev.h, ev.v)
                    sd.inc(nc.sync.dma_start(out=xt9[:, b, :],
                                             in_=x_in[row0:row0 + 128, :]), 16)
                    ldv = sd.v
                    for g in range(4):
                        psb = psT9[bi9 % 2]
                        if bi9 >= 2:
                            nc.tensor.wait_ge(ev.h, ev.v - 1)
                        for h8 in range(8):
                            hc = 8 * g + h8
                            ins = nc.tensor.transpose(
                                psb[:, 128 * h8:128 * h8 + 128],
                                bass.AP(OutAcc, hc * 512 + 128 * tt,
                                        [[NE * 512, 128], [1, 128]]),
                                identh[:, :])
                        pt.inc(ins)
                        nc.vector.wait_ge(pt.h, pt.v)
                        nc.vector.wait_ge(sd.h, ldv)
                        if g == 0 and tt >= 2:
                            nc.vector.wait_ge(t_st.h, store_after[tt - 2])
                        ev.inc(nc.vector.tensor_tensor(
                            stb[:, b, 1024 * g:1024 * g + 1024],
                            bass.AP(psb, 0, [[1024, 128], [1, 1024]]),
                            xt9[:, b, 1024 * g:1024 * g + 1024], ALU.add))
                        bi9 += 1
                    nc.sync.wait_ge(ev.h, ev.v)
                    t_st.inc(nc.sync.dma_start(out=out[row0:row0 + 128, :],
                                               in_=stb[:, b, :]), 16)
                    store_after[tt] = t_st.v
                nc.sync.wait_ge(t_st.h, t_st.v)
                nc.all_engine_barrier()

    nc.sync.wait_ge(t_st.h, t_st.v)
    big.close()
    return nc


def kernel(L0_attn_hidden_states, L0_attn_w_qkv, L0_attn_w_c, L0_ffn_ln_w,
           L0_ffn_gate_w, L0_ffn_up_w, L0_ffn_down_w):
    x = np.ascontiguousarray(np.asarray(L0_attn_hidden_states, dtype=np.float32)[0])
    wqkv = np.ascontiguousarray(np.asarray(L0_attn_w_qkv, dtype=np.float32))
    wc = np.ascontiguousarray(np.asarray(L0_attn_w_c, dtype=np.float32))
    lnw = np.ascontiguousarray(np.asarray(L0_ffn_ln_w, dtype=np.float32))
    w1 = np.ascontiguousarray(np.asarray(L0_ffn_gate_w, dtype=np.float32))
    w2 = np.ascontiguousarray(np.asarray(L0_ffn_up_w, dtype=np.float32))
    w3 = np.ascontiguousarray(np.asarray(L0_ffn_down_w, dtype=np.float32))

    nc = build_nc()
    in_maps = []
    for c in range(NCORE):
        in_maps.append({
            "x_own": np.ascontiguousarray(x[TOK * c:TOK * (c + 1)]),
            "w_qkv": wqkv, "w_c": wc, "ln_w": lnw,
            "gate_w": w1, "up_w": w2, "down_w": w3,
            "hids": np.array([[c, c + 8, c + 16, c + 24]], dtype=np.float32),
        })
    res = run_bass_kernel_spmd(nc, in_maps, core_ids=list(range(NCORE)))
    out = np.concatenate([np.asarray(res.results[c]["out"]) for c in range(NCORE)], axis=0)
    return out.reshape(1, S, E).astype(np.float32)
